# version 20
# baseline (speedup 1.0000x reference)
import contextlib
import numpy as np
import concourse.bass as bass
import concourse.mybir as mybir
from concourse.bass_utils import run_bass_kernel_spmd

F32 = mybir.dt.float32
BF16 = mybir.dt.bfloat16
AF = mybir.ActivationFunctionType
ALU = mybir.AluOpType
AX = mybir.AxisListType.X

D = 1024
DC = 8
TOK = 2048
HALO = 128
XR = TOK + HALO
NT = 16
NU = 32
NBK = 16
NP = 6
GH = 12
HH = 2048
HFC = 16
NEXP = 9
TH = 1024
NEG = -1.0e30

ENGINES = ("sync", "gpsimd", "tensor", "vector", "scalar")


class Sched:
    def __init__(self):
        self.ops = {e: [] for e in ENGINES}
        self.cnt = {}
        self.vc = {e: {} for e in ENGINES}

    def add(self, eng, fn, waits=(), inc=None, amt=1, dma=False):
        w2 = []
        for t in waits:
            if t is None:
                continue
            s, v = t
            if self.vc[eng].get(s, 0) < v:
                w2.append((s, v))
                self.vc[eng][s] = v
        tok = None
        if inc is not None:
            self.cnt[inc] = self.cnt.get(inc, 0) + amt
            tok = (inc, self.cnt[inc])
        self.ops[eng].append((fn, w2, inc, amt, dma))
        return tok


def build(dbg=(), phases=99):
    nc = bass.Bass("TRN2", target_bir_lowering=False,
                   detect_race_conditions=False)

    x_in = nc.dram_tensor("x_in", [XR, D], F32, kind="ExternalInput")
    wq_in = nc.dram_tensor("wq_in", [D, D], F32, kind="ExternalInput")
    wk_in = nc.dram_tensor("wk_in", [D, D], F32, kind="ExternalInput")
    wv_in = nc.dram_tensor("wv_in", [D, D], F32, kind="ExternalInput")
    wo_in = nc.dram_tensor("wo_in", [D, D], F32, kind="ExternalInput")
    bo_in = nc.dram_tensor("bo_in", [1, D], F32, kind="ExternalInput")
    rw_in = nc.dram_tensor("rw_in", [D, 8], F32, kind="ExternalInput")
    rtb_in = nc.dram_tensor("rtb_in", [1, 8], F32, kind="ExternalInput")
    sw1_in = nc.dram_tensor("sw1_in", [D, HH], F32, kind="ExternalInput")
    sw2_in = nc.dram_tensor("sw2_in", [HH, D], F32, kind="ExternalInput")
    rw1_in = nc.dram_tensor("rw1_in", [8, D, HH], F32, kind="ExternalInput")
    rw2_in = nc.dram_tensor("rw2_in", [8, HH, D], F32, kind="ExternalInput")
    rb1c_in = nc.dram_tensor("rb1c_in", [128, NEXP * HFC], F32, kind="ExternalInput")
    rb9_in = nc.dram_tensor("rb9_in", [NEXP, D], F32, kind="ExternalInput")
    ident_in = nc.dram_tensor("ident_in", [128, 128], F32, kind="ExternalInput")
    esum_in = nc.dram_tensor("esum_in", [128, NP * 16], F32, kind="ExternalInput")
    ebc_in = nc.dram_tensor("ebc_in", [GH, NP * 128], F32, kind="ExternalInput")
    maskr_in = nc.dram_tensor("maskr_in", [128, 512], F32, kind="ExternalInput")
    mask0_in = nc.dram_tensor("mask0_in", [128, 512], F32, kind="ExternalInput")
    pm_in = nc.dram_tensor("pm_in", [128, 4], F32, kind="ExternalInput")

    y_out = nc.dram_tensor("y", [TOK, D], F32, kind="ExternalOutput")

    qgd = nc.dram_tensor("qgd", [128, NP, TOK], BF16)
    sd = nc.dram_tensor("sd", [128, NU, 384], BF16)
    ccin = nc.dram_tensor("ccin", [128, 392], F32)
    ccout = nc.dram_tensor("ccout", [512, 392], F32)

    ctx = contextlib.ExitStack()
    with ctx:
        ST = lambda n, s, d: ctx.enter_context(nc.sbuf_tensor(n, s, d))
        PT = lambda n, s, d: ctx.enter_context(nc.psum_tensor(n, s, d))

        wstream = ST("wstream", [128, DC, 2048], BF16)
        slotB = ST("slotB", [128, HFC, D], BF16)
        xh = ST("xh", [128, DC, XR], BF16)
        acc = ST("acc", [128, 8, D], F32)
        attnloc = ST("attnloc", [128, 2, TOK], BF16)
        SCRN = 26688
        scr = ST("scr", [128, SCRN], BF16)
        zst = ST("zst", [128, NU, NP], F32)
        pzsb = ST("pzsb", [128, 392], F32)
        pz0 = ST("pz0", [128, 390], F32)
        wts = ST("wts", [128, NT, 8], F32)
        idb = ST("idb", [128, 128], BF16)
        maskr = ST("maskr", [128, 512], BF16)
        mask0 = ST("mask0", [128, 512], BF16)
        pm = ST("pm", [128, 4], F32)
        esum = ST("esum", [128, NP, 16], BF16)
        ebc = ST("ebc", [GH, NP, 128], F32)
        onesrow = ST("onesrow", [1, 128], BF16)
        onescol = ST("onescol", [128, 1], BF16)
        bo_b = ST("bo_b", [1, D], BF16)
        rb9 = ST("rb9", [NEXP, D], BF16)
        rtb = ST("rtb", [1, 8], BF16)
        rwq = ST("rwq", [128, DC, 8], BF16)
        rb1c = ST("rb1c", [128, NEXP, HFC], F32)
        epsln = ST("epsln", [128, 1], F32)
        waugb = ST("waugb", [128, 9], BF16)
        wt9 = ST("wt9", [9, 8, 128], BF16)
        lg = ST("lg", [128, 8], F32)
        lgs = ST("lgs", [128, 8, 8], F32)
        lsum = ST("lsum", [128, 8], F32)
        stc = ST("stc", [128, 16], F32)
        lac = ST("lac", [128, 8], F32)
        lns = ST("lns", [128, 8, 2], F32)
        rz_t = ST("rz_t", [128, 32], F32)

        def V(off, shape, dt=BF16):
            n = int(np.prod(shape))
            cols = n if dt == BF16 else 2 * n
            assert off + cols <= SCRN, (off, cols)
            ap = scr[:, off:off + cols]
            if dt == F32:
                ap = ap.bitcast(F32)
            if len(shape) > 1:
                names = "abcdef"[:len(shape)]
                ap = ap.rearrange("p (%s) -> p %s" % (" ".join(names), " ".join(names)),
                                  **dict(zip(names, shape)))
            return ap

        R0 = 0
        R1 = 4352
        R2 = 8704
        R3 = 12800
        R4 = 13824
        R5 = 15872
        R6 = 20480
        R7 = 23552

        kloc = V(R0, [2, XR])
        r1b = V(R0, [2, D], F32)
        vloc = V(R1, [17, 256])
        xch = V(R1, [2, D], F32)
        qloc = V(R2, [2, TOK])
        attT = V(R3, [2, 2, 2, 128])
        attb = V(R6, [2, 512])
        xstg = V(R4, [2, D])
        scb = V(R4, [2, 512], F32)
        hcst = V(R4, [2, D])
        qgst = V(R5, [NP, 512])
        qgrot = V(R5, [2, NP, 256])
        sinr = V(R5 + 3072, [2, 2, 384])
        h1 = V(R5, [HFC, 512])
        kgrot = V(R6, [2, 768])
        vgrot = V(R6 + 1536, [2, 768])
        attnw = V(R6, [2, NP, 128])
        ctxb = V(R6 + 1536, [2, 384])
        elust = V(R7, [512], F32)
        sstg = V(R7 + 1024, [4, 384])
        ccst = V(R7, [4, 392], F32)
        rsumb = attnloc[:, 0, 0:1024].bitcast(F32)
        sqdump = V(R2, [512], F32)
        dsnap = V(R7, [4, 392], F32) if "snap" in dbg else None

        pD01 = PT("pd01", [128, 1024], F32)
        pD23 = PT("pd23", [128, 1024], F32)
        p = [pD01[:, 0:512], pD01[:, 512:1024], pD23[:, 0:512], pD23[:, 512:1024],
             PT("ps4", [128, 512], F32), PT("ps5", [128, 512], F32)]
        pts = [PT("pt0", [128, 1024], BF16), PT("pt1", [128, 1024], BF16)]

        sems = {}
        for s in ("sD", "sGd", "sGc", "sT", "sV", "sS", "sC", "sDp", "sGp"):
            sems[s] = ctx.enter_context(nc.semaphore(s))

        S = Sched()
        bankg = {}
        dbg_specs = []
        ndma = {"sync": 0, "gpsimd": 0}
        QSEM = {"sync": "sD", "gpsimd": "sGd"}
        QPUB = {"sync": "sDp", "gpsimd": "sGp"}

        def dma(queue, out_ap, in_ap, waits=()):
            ndma[queue] += 1
            S.add(queue, lambda eng, o=out_ap, i=in_ap: eng.dma_start(o, i),
                  waits=waits, inc=QSEM[queue], amt=16, dma=True)

        def publish(queue):
            return S.add(queue, lambda eng: eng.nop(),
                         waits=[(QSEM[queue], 16 * ndma[queue])], inc=QPUB[queue])

        def mm(groups, waits=(), pe_bank=None):
            w = list(waits)
            if pe_bank is not None:
                g = bankg.get(pe_bank)
                if isinstance(g, list):
                    w.extend(g)
                else:
                    w.append(g)
            def fn(eng, groups=groups):
                ins = None
                for items in groups:
                    n = len(items)
                    for k, (o, l, r) in enumerate(items):
                        ins = eng.matmul(o, l, r, start=(k == 0), stop=(k == n - 1))
                return ins
            return S.add("tensor", fn, waits=w, inc="sT")

        def vop(fn, waits=()):
            return S.add("vector", fn, waits=waits, inc="sV")

        def sop(fn, waits=()):
            return S.add("scalar", fn, waits=waits, inc="sS")

        def gop(fn, waits=()):
            return S.add("gpsimd", fn, waits=waits, inc="sGc")

        def finish():
            S.add("sync", lambda eng: eng.nop(),
                  waits=[("sD", 16 * ndma["sync"]), ("sGd", 16 * ndma["gpsimd"])])
            with nc.Block() as blk:
                for ename in ENGINES:
                    ops = S.ops[ename]
                    drn = ename in ()
                    def body(eng, ops=ops, drn=drn):
                        for fn, waits, inc, amt, isdma in ops:
                            for (s2, v2) in waits:
                                eng.wait_ge(sems[s2], v2)
                            ins = fn(eng)
                            if inc is not None:
                                ins.then_inc(sems[inc], amt)
                            if drn and not isdma:
                                eng.drain()
                    getattr(blk, ename)(body)
            return nc

        def tap(name, ap, waits):
            if name not in dbg:
                return
            if callable(getattr(ap, "ap", None)):
                ap = ap.ap()
            cnts = [cnt for _, cnt in ap.ap]
            total = int(np.prod(cnts[1:])) if len(cnts) > 1 else 1
            o = nc.dram_tensor("dbg_" + name, [cnts[0], total], ap.dtype,
                              kind="ExternalOutput")
            dbg_specs.append(name)
            flat_o = o[:, :]
            if len(cnts) > 2:
                names = "abcdef"[:len(cnts) - 1]
                flat_o = flat_o.rearrange(
                    "p (%s) -> p %s" % (" ".join(names), " ".join(names)),
                    **dict(zip(names, cnts[1:])))
            ndma["sync"] += 1
            S.add("sync", lambda eng, o=flat_o, ap=ap: eng.dma_start(o, ap),
                  waits=waits, inc="sD", amt=16, dma=True)

        dma("gpsimd", idb[:, :], ident_in[:, :])
        t_xevac = {}
        t_xpub = {}
        xstg8 = attnloc[:, :, :].rearrange("p a b -> p (a b)").rearrange(
            "p (s b) -> p s b", s=4)

        def emit_xtr(tc):
            ptb = pts[tc % 2]
            def tr_fn(eng, ptb=ptb, tc=tc):
                ins = None
                for dc2 in range(DC):
                    ins = eng.transpose(ptb[:, 128 * dc2:128 * dc2 + 128],
                                        xstg8[:, tc % 4, 128 * dc2:128 * dc2 + 128],
                                        idb[:, :])
                return ins
            wkey = (tc, "c") if tc < 4 else tc // 4
            t_tr = S.add("tensor", tr_fn,
                         waits=[t for t in (t_xpub[wkey],
                                            bankg.get(("pt", tc % 2))) if t],
                         inc="sT")
            def ev_fn(eng, ptb=ptb, tc=tc):
                ins = None
                for dc2 in range(DC):
                    ins = eng.copy(xh[:, dc2, 128 * tc:128 * tc + 128],
                                   ptb[:, 128 * dc2:128 * dc2 + 128])
                return ins
            t_ev = sop(ev_fn, waits=[t_tr])
            bankg[("pt", tc % 2)] = t_ev
            t_xevac[tc] = t_ev

        for tc in range(4):
            dma("sync", acc[:, tc, :], x_in[128 * tc:128 * tc + 128, :])
        t_x03 = publish("sync")
        for tc in range(4):
            t_xpub[(tc, "c")] = vop(
                lambda eng, tc=tc: eng.tensor_copy(xstg8[:, tc, :], acc[:, tc, :]),
                waits=[t_x03])
        for tc in range(0, 4):
            emit_xtr(tc)
        for tc in range(4, 8):
            dma("gpsimd", xstg8[:, tc % 4, :], x_in[128 * tc:128 * tc + 128, :],
                waits=[t_xevac[tc - 4]])
        t_xpub[1] = publish("gpsimd")
        for tc in range(4, 8):
            emit_xtr(tc)
        for grp in (2, 3, 4):
            lo = 4 * grp
            hi = min(17, lo + 4)
            for tc in range(lo, hi):
                dma("gpsimd", xstg8[:, tc % 4, :],
                    x_in[128 * tc:128 * tc + 128, :], waits=[t_xevac[tc - 4]])
            t_xpub[grp] = publish("gpsimd")
            for tc in range(lo, hi):
                emit_xtr(tc)
        t_xT = t_xevac[16]

        dma("gpsimd", wstream[:, :, 0:1024],
            wq_in[:, :].rearrange("(c p) j -> p c j", p=128))
        dma("gpsimd", wstream[:, :, 1024:2048],
            wk_in[:, :].rearrange("(c p) j -> p c j", p=128))
        t_wq = t_wk = publish("gpsimd")
        for dst, src in [
            (esum[:, :, :], esum_in[:, :].rearrange("p (a b) -> p a b", b=16)),
            (ebc[:, :, :], ebc_in[:, :].rearrange("p (a b) -> p a b", b=128)),
            (maskr[:, :], maskr_in[:, :]),
            (mask0[:, :], mask0_in[:, :]),
            (bo_b[:, :], bo_in[:, :]),
            (rb9[:, :], rb9_in[:, :]),
            (rtb[:, :], rtb_in[:, :]),
            (rwq[:, :, :], rw_in[:, :].rearrange("(c p) j -> p c j", p=128)),
        ]:
            dma("gpsimd", dst, src)
        t_consts = publish("gpsimd")
        dma("sync", pm[:, :], pm_in[:, :])
        dma("sync", rb1c[:, :, :],
            rb1c_in[:, :].rearrange("p (a b) -> p a b", b=HFC))
        t_pm = t_rb1c = publish("sync")
        t_ms = gop(lambda eng: (eng.memset(onesrow[:, :], 1.0),
                                eng.memset(onescol[:, :], 1.0),
                                eng.memset(epsln[:, :], 1e-5),
                                eng.memset(waugb[:, 8:9], 1.0),
                                eng.memset(pzsb[:, :], 0.0),
                                eng.memset(pz0[:, :], 0.0))[-1])

        if phases < 2:
            return finish()

        t_qg_dma = None
        t_qloc_done = None
        t_qmm_last = None
        for tt in range(4):
            c0 = HALO + 512 * tt
            ev = []
            for jc in range(8):
                bk = jc % 2
                t_qmm_last = mm([[(p[bk][:, :],
                                   wstream[:, dc, 128 * jc:128 * jc + 128],
                                   xh[:, dc, c0:c0 + 512]) for dc in range(DC)]],
                                waits=[t_wq, t_xevac[min(16, 4 + 4 * tt)]], pe_bank=bk)
                if jc < 2:
                    t_e = sop(lambda eng, bk=bk, jc=jc, tt=tt:
                              eng.copy(qloc[:, jc, 512 * tt:512 * tt + 512],
                                       p[bk][:, :]),
                              waits=[t_qmm_last])
                    t_qloc_done = t_e
                else:
                    t_e = sop(lambda eng, bk=bk, jc=jc:
                              eng.activation(qgst[:, jc - 2, :], p[bk][:, :], AF.Exp),
                              waits=[t_qmm_last, t_qg_dma])
                bankg[bk] = t_e
                ev.append(t_e)
            t_sum = mm([[(p[2][0:12, :], esum[:, ci, 0:12], qgst[:, ci, :])
                         for ci in range(NP)]], waits=ev[2:] + [t_consts], pe_bank=2)
            t_rec = vop(lambda eng: (eng.reciprocal(rsumb[0:12, :], p[2][0:12, :]),
                                     eng.drain())[-1], waits=[t_sum, t_xT])
            bankg[2] = t_rec
            mul_toks = []
            for ci in range(NP):
                bk = 3 + ci % 2
                t_bc = mm([[(p[bk][:, :], ebc[0:12, ci, :], rsumb[0:12, :])]],
                          waits=[t_rec], pe_bank=bk)
                t_mu = vop(lambda eng, bk=bk, ci=ci:
                           eng.tensor_mul(qgst[:, ci, :], qgst[:, ci, :], p[bk][:, :]),
                           waits=[t_bc])
                bankg[bk] = t_mu
                mul_toks.append(t_mu)
            dma("sync", qgd[:, :, 512 * tt:512 * tt + 512], qgst[:, :, :],
                waits=mul_toks)
            t_qg_dma = publish("sync")
            if tt == 0:
                tap("qgst0", qgst[:, :, :], mul_toks)

        if phases < 2.05:
            return finish()

        kl_slices = [(0, 128)] + [(128 + 512 * k, 512) for k in range(4)]
        ks_i = 0
        t_kloc_done = None
        for jc in range(2):
            for (cs, cn) in kl_slices:
                bk = ks_i % 2
                t_km = mm([[(p[bk][:, 0:cn],
                             wstream[:, dc, 1024 + 128 * jc:1024 + 128 * jc + 128],
                             xh[:, dc, cs:cs + cn]) for dc in range(DC)]],
                          waits=[t_wk, t_xevac[min(16, (cs + cn - 1) // 128)]],
                          pe_bank=bk)
                t_e = sop(lambda eng, bk=bk, jc=jc, cs=cs, cn=cn:
                          eng.copy(kloc[:, jc, cs:cs + cn], p[bk][:, 0:cn]),
                          waits=[t_km])
                bankg[bk] = t_e
                t_kloc_done = t_e
                ks_i += 1
        tap("kloc", kloc, [t_kloc_done])
        tap("qloc", qloc, [t_qloc_done])

        if phases < 2.4:
            return finish()
        dma("gpsimd", wstream[:, :, 0:1024],
            wv_in[:, :].rearrange("(c p) j -> p c j", p=128),
            waits=[t_qmm_last])
        t_wv = publish("gpsimd")
        dma("gpsimd", slotB[:, :, :],
            sw2_in[:, :].rearrange("(c p) j -> p c j", p=128))
        t_sw2 = publish("gpsimd")

        t_hmv = mm([[(p[2][:, :], xh[:, dc, 0:128], wstream[:, dc, 0:512])
                     for dc in range(DC)]], waits=[t_wv, t_xevac[0]], pe_bank=2)
        t_ev0 = sop(lambda eng: eng.copy(vloc[:, 0, :], p[2][:, 0:256]),
                    waits=[t_hmv])
        bankg[2] = t_ev0
        t_vloc = {0: t_ev0}

        if phases < 2.35:
            return finish()
        t_pz = None
        t_sz = {}
        t_sd_dma = {}
        sstg_tok = {}
        t_zst = {}
        t_elu_guard = None
        import os as _os
        _ntc = int(_os.environ.get("KNTC", "17"))
        for tc in range(1, min(_ntc, 17) if phases >= 2.38 else 2):
            tci = tc - 1
            xc0 = 128 * tc
            w_rot = [t_sz.get(tc - 2)]
            t_vg = []
            for js in range(2):
                bk = 2 + js
                t_vm = mm([[(p[bk][:, :], xh[:, dc, xc0:xc0 + 128],
                             wstream[:, dc, 512 * js:512 * js + 512])
                            for dc in range(DC)]],
                          waits=[t_wv, t_xevac[tc]], pe_bank=bk)
                if js == 0:
                    t_e2 = vop(lambda eng, bk=bk, tc=tc: (
                        eng.tensor_copy(vloc[:, tc, :], p[bk][:, 0:256]),
                        eng.tensor_copy(vgrot[:, tc % 2, 0:256],
                                        p[bk][:, 256:512]))[-1],
                        waits=[t_vm] + w_rot)
                    bankg[bk] = t_e2
                    t_vloc[tc] = t_e2
                    t_vg.append(t_e2)
                else:
                    t_e2 = vop(lambda eng, bk=bk, tc=tc:
                               eng.tensor_copy(vgrot[:, tc % 2, 256:768], p[bk][:, :]),
                               waits=[t_vm] + w_rot)
                    bankg[bk] = t_e2
                    t_vg.append(t_e2)
            if phases < 2.6:
                continue
            t_kg = []
            for js, (j0, jn, g0) in enumerate([(256, 512, 0), (768, 256, 512)]):
                bk = 4 + js
                t_kmm = mm([[(p[bk][:, 0:jn], xh[:, dc, xc0:xc0 + 128],
                              wstream[:, dc, 1024 + j0:1024 + j0 + jn])
                             for dc in range(DC)]],
                           waits=[t_wk, t_xevac[tc]], pe_bank=bk)
                t_m1 = vop(lambda eng, bk=bk, jn=jn:
                           eng.tensor_scalar_min(elust[:, 0:jn], p[bk][:, 0:jn], 0.0),
                           waits=[t_kmm, t_elu_guard])
                t_m2 = sop(lambda eng, jn=jn:
                           eng.activation(elust[:, 0:jn], elust[:, 0:jn], AF.Exp),
                           waits=[t_m1])
                t_m3 = vop(lambda eng, bk=bk, jn=jn, g0=g0, tc=tc:
                           eng.scalar_tensor_tensor(kgrot[:, tc % 2, g0:g0 + jn],
                                                    p[bk][:, 0:jn], 0.0,
                                                    elust[:, 0:jn], ALU.max, ALU.add),
                           waits=[t_m2] + w_rot)
                t_elu_guard = t_m3
                bankg[bk] = t_m3
                t_kg.append(t_m3)
            if phases < 2.8:
                continue
            for uu in range(2):
                u = 2 * tci + uu
                bk = uu
                rows = slice(64 * uu, 64 * uu + 64)
                groups = []
                for pr in range(NP):
                    ke = kgrot[rows, tc % 2, 128 * pr:128 * pr + 64]
                    ko = kgrot[rows, tc % 2, 128 * pr + 64:128 * pr + 128]
                    ve = vgrot[rows, tc % 2, 128 * pr:128 * pr + 64]
                    vo = vgrot[rows, tc % 2, 128 * pr + 64:128 * pr + 128]
                    groups.append([(p[bk][0:64, 64 * pr:64 * pr + 64], ke, ve)])
                    groups.append([(p[bk][0:64, 384 + pr:385 + pr], ke,
                                    onescol[rows, :])])
                    groups.append([(p[bk][64:128, 64 * pr:64 * pr + 64], ko, vo)])
                    groups.append([(p[bk][64:128, 384 + pr:385 + pr], ko,
                                    onescol[rows, :])])
                t_smm = mm(groups, waits=t_vg + t_kg + [t_ms], pe_bank=bk)
                t_sz[tc] = t_smm
                t_pz = vop(lambda eng, bk=bk:
                           eng.tensor_add(pzsb[:, 0:390], pzsb[:, 0:390], p[bk][:, 0:390]),
                           waits=[t_smm])
                t_cp = vop(lambda eng, bk=bk, u=u: (
                    eng.tensor_copy(sstg[:, u % 4, :], p[bk][:, 0:384]),
                    eng.tensor_copy(zst[:, u, :], p[bk][:, 384:390]))[-1],
                    waits=[t_smm, t_sd_dma.get(u - 4)])
                t_zst[u] = t_cp
                bankg[bk] = t_cp
                sstg_tok[u] = t_cp
                if u % 4 == 3 and phases >= 2.9:
                    dma("sync", sd[:, u - 3:u + 1, :], sstg[:, :, :],
                        waits=[sstg_tok[u - k] for k in range(4)] + [t_zst[u]])
                    t_d = publish("sync")
                    for k in range(4):
                        t_sd_dma[u - 3 + k] = t_d
        tap("pzsb", pzsb[:, 0:390], [t_pz])
        tap("sdd", sd[:, :, :], [t_sd_dma[NU - 1]])
        tap("qgdd", qgd[:, :, :], [t_qg_dma])
        tap("vloc", vloc, [t_vloc[16]])

        if phases < 4:
            return finish()

        dma("sync", ccin[:, :], pzsb[:, :], waits=[t_pz])
        t_ccin = publish("sync")
        t_cc = S.add("gpsimd",
                     lambda eng: eng.collective_compute(
                         "AllGather", ALU.bypass,
                         replica_groups=[[0, 1, 2, 3], [4, 5, 6, 7]],
                         ins=[ccin.ap().opt()], outs=[ccout.ap().opt()]),
                     waits=[t_ccin], inc="sC", amt=1, dma=True)
        dma("gpsimd", ccst[:, :, :],
            ccout[:, :].rearrange("(r p) f -> p r f", p=128),
            waits=[t_cc, t_sd_dma[NU - 1], t_elu_guard])
        t_ccb = publish("gpsimd")

        dma("gpsimd", wstream[:, :, 1024:2048],
            wo_in[:, :].rearrange("(c p) j -> p c j", p=128),
            waits=[t_sz[16]])
        t_wo0 = publish("gpsimd")

        if phases < 5:
            return finish()

        att_tok = {}
        t_p4_last = None
        pair_state = {}

        def p4_front(idx):
            i, hp = idx // 2, idx % 2
            sl = idx % 2
            pdd = pD23 if sl else pD01
            bka, bkb = (2, 3) if sl else (0, 1)
            t_s = mm([[(pdd[:, 0:256],
                        qloc[0:64, hp, 128 * i:128 * i + 128],
                        kloc[0:64, hp, 128 * i:128 * i + 256])],
                      [(pdd[:, 512:768],
                        qloc[64:128, hp, 128 * i:128 * i + 128],
                        kloc[64:128, hp, 128 * i:128 * i + 256])]],
                     waits=[t_qloc_done, t_kloc_done, bankg.get(bka),
                            bankg.get(bkb)])
            msk = mask0 if i == 0 else maskr
            t_a = vop(lambda eng, pdd=pdd, sl=sl, msk=msk:
                      eng.tensor_add(
                          scb[:, sl, :].rearrange("p (a b) -> p a b", b=256),
                          pdd[:, :].rearrange("p (a b) -> p a b", b=512)[:, :, 0:256],
                          msk[:, :].rearrange("p (a b) -> p a b", b=256)),
                      waits=[t_s, att_tok.get(("sc", sl)), t_consts])
            bankg[bka] = t_a
            bankg[bkb] = t_a
            t_x = sop(lambda eng, sl=sl: (
                eng.activation(scb[:, sl, 0:256], scb[:, sl, 0:256], AF.Exp,
                               scale=0.125, accum_out=lac[:, 4 * sl:4 * sl + 1]),
                eng.drain(),
                eng.activation(scb[:, sl, 256:512], scb[:, sl, 256:512], AF.Exp,
                               scale=0.125,
                               accum_out=lac[:, 4 * sl + 1:4 * sl + 2]))[-1],
                waits=[t_a, att_tok.get(("lac", sl))])
            pair_state[idx] = t_x

        def p4_back(idx):
            nonlocal t_p4_last
            i, hp = idx // 2, idx % 2
            sl = idx % 2
            t_x = pair_state.pop(idx)
            t_r = vop(lambda eng, sl=sl: (
                eng.reciprocal(lac[:, 4 * sl + 2:4 * sl + 4],
                               lac[:, 4 * sl:4 * sl + 2]),
                eng.drain(),
                eng.tensor_tensor(
                    attb[:, sl, :].rearrange("p (a b) -> p a b", b=256),
                    scb[:, sl, :].rearrange("p (a b) -> p a b", b=256),
                    bass.AP(lac, 4 * sl + 2, [[8, 128], [1, 2], [0, 256]]),
                    ALU.mult))[-1],
                waits=[t_x, att_tok.get(("attb", sl)), t_sz[16]])
            att_tok[("sc", sl)] = t_r
            att_tok[("lac", sl)] = t_r
            ptb = pts[sl]
            t_t = S.add("tensor", lambda eng, ptb=ptb, sl=sl: [
                eng.transpose(ptb[:, 128 * q2:128 * q2 + 128],
                              attb[:, sl, 128 * q2:128 * q2 + 128], idb[:, :])
                for q2 in range(4)][-1],
                waits=[t for t in (t_r, bankg.get(("pt", sl))) if t], inc="sT")
            att_tok[("attb", sl)] = t_t
            t_te = sop(lambda eng, ptb=ptb, sl=sl: [
                eng.copy(attT[:, sl, q2 // 2, q2 % 2, :],
                         ptb[:, 128 * q2:128 * q2 + 128])
                for q2 in range(4)][-1],
                waits=[t_t, att_tok.get(("attT", sl))])
            bankg[("pt", sl)] = t_te
            bk2 = 4 + sl
            t_av = mm([[(p[bk2][0:64, 0:128],
                         vloc[:, i + kc, 128 * hp:128 * hp + 64],
                         attT[:, sl, 0, kc, :]) for kc in range(2)],
                       [(p[bk2][64:128, 0:128],
                         vloc[:, i + kc, 128 * hp + 64:128 * hp + 128],
                         attT[:, sl, 1, kc, :]) for kc in range(2)]],
                      waits=[t_te, t_vloc[i], t_vloc[i + 1]], pe_bank=bk2)
            att_tok[("attT", sl)] = t_av
            t_ae = sop(lambda eng, bk2=bk2, hp=hp, i=i:
                       eng.copy(attnloc[:, hp, 128 * i:128 * i + 128],
                                p[bk2][:, 0:128]),
                       waits=[t_av])
            bankg[bk2] = t_ae
            t_p4_last = t_ae

        for k in range(2 * NBK + 1):
            if k < 2 * NBK:
                p4_front(k)
            if k >= 1:
                p4_back(k - 1)
        tap("attnloc", attnloc, [t_p4_last])

        t_pfx = None
        for r in range(4):
            t_pfx = vop(lambda eng, r=r: (
                        eng.scalar_tensor_tensor(pz0[:, :], ccst[:, r, 0:390],
                                                 pm[:, r:r + 1], pz0[:, :],
                                                 ALU.mult, ALU.add),
                        eng.drain())[-1],
                        waits=[t_ccb, t_ms, t_pm])
        tap("pz0", pz0[:, :], [t_pfx])

        if phases < 6:
            return finish()

        rz_bcast = [bass.AP(rz_t, 8, [[32, 128], [1, NP], [0, 64]]),
                    bass.AP(rz_t, 24, [[32, 128], [1, NP], [0, 64]])]
        t_womm_last = {}
        t_prod_lo = {}
        t_prod_all = {}
        t_cons_lo = {}
        t_cons_all = {}
        t_he = {}
        t_sm_tok = {}
        t_we9 = {}
        t_y = {}
        r1_guard = {}
        xch_guard = {}
        stc_guard = None
        lg_guard = None
        waug_guard = None
        hc_guard = {}
        t_norm = {}
        t_h1_guard = None
        t_qguse = {}
        t_sinuse = {}
        t_w1 = {}
        t_w2 = {}
        t_wo1_tok = None

        for hf in range(2):
            t_qg_res = {}
            t_sin_res = {}

            def emit_restores(b, w_extra=()):
                u0 = 16 * hf + 4 * b
                w = [t_qg_dma, t_qguse.get((hf, b - 2))] + list(w_extra)
                dma("sync", qgrot[:, b % 2, :, :],
                    qgd[:, :, 64 * u0:64 * u0 + 256], waits=w)
                for sb in (2 * b, 2 * b + 1):
                    u1 = 16 * hf + 2 * sb
                    w2 = [t_sd_dma[min(NU - 1, u1 + 1)],
                          t_sinuse.get((hf, sb - 2))] + list(w_extra)
                    dma("sync", sinr[:, sb % 2, :, :],
                        sd[:, u1:u1 + 2, :], waits=w2)
                t_qg_res[b] = t_sin_res[2 * b] = t_sin_res[2 * b + 1] = publish("sync")

            def emit_qg_restore(b, w_extra=()):
                u0 = 16 * hf + 4 * b
                w = [t_qg_dma, t_qguse.get((hf, b - 2))] + list(w_extra)
                dma("sync", qgrot[:, b % 2, :, :],
                    qgd[:, :, 64 * u0:64 * u0 + 256], waits=w)
                t_qg_res[b] = publish("sync")

            def emit_sin_restore(sb, w_extra=()):
                u1 = 16 * hf + 2 * sb
                w2 = [t_sd_dma[min(NU - 1, u1 + 1)],
                      t_sinuse.get((hf, sb - 2))] + list(w_extra)
                dma("sync", sinr[:, sb % 2, :, :],
                    sd[:, u1:u1 + 2, :], waits=w2)
                t_sin_res[sb] = publish("sync")

            w_extra = [t_cons_all.get((0, 8))] if hf == 1 else []
            for b_ in range(2):
                u0 = 16 * hf + 4 * b_
                dma("sync", qgrot[:, b_ % 2, :, :],
                    qgd[:, :, 64 * u0:64 * u0 + 256],
                    waits=[t_qg_dma] + w_extra)
            for sb_ in range(2):
                u1 = 16 * hf + 2 * sb_
                dma("sync", sinr[:, sb_ % 2, :, :], sd[:, u1:u1 + 2, :],
                    waits=[t_sd_dma[min(NU - 1, u1 + 1)]] + w_extra)
            t_qg_res[0] = t_qg_res[1] = t_sin_res[0] = t_sin_res[1] =                 publish("sync")

            wo_ap = (lambda dc, c: wstream[:, dc, 1024 + c[0]:1024 + c[1]]) if hf == 0 \
                else (lambda dc, c: slotB[:, dc, c[0]:c[1]])
            t_wo_tok = t_wo0 if hf == 0 else t_wo1_tok

            t_we_prev = {}
            t_cm_hist = {}

            def emit_ctx(li2, waits):
                sl = li2 % 2
                return vop(lambda eng, sl=sl: (
                    eng.tensor_scalar_add(rz_t[:, 16 * sl:16 * sl + 6],
                                          pz0[:, 384:390], 1e-6),
                    eng.drain(),
                    eng.reciprocal(rz_t[:, 16 * sl + 8:16 * sl + 14],
                                   rz_t[:, 16 * sl:16 * sl + 6]),
                    eng.drain(),
                    eng.tensor_tensor(
                        ctxb[:, sl, :].rearrange("p (a b) -> p a b", b=64),
                        pz0[:, 0:384].rearrange("p (a b) -> p a b", b=64),
                        rz_bcast[sl], ALU.mult))[-1],
                    waits=waits)

            w = [t_pfx, t_sz[16]] if hf == 0 else [t_cons_all[(0, 8)]]
            t_ctx_tok = {0: emit_ctx(0, w)}
            for li in range(16):
                u = 16 * hf + li
                tc2 = li // 2
                tcg = 8 * hf + tc2
                b = li // 4
                sl = li % 2
                pdd = pD01 if li % 2 == 0 else pD23
                bka, bkb = (0, 1) if li % 2 == 0 else (2, 3)
                groups = []
                for pr in range(NP):
                    groups.append([(pdd[0:64, 64 * pr:64 * pr + 64],
                                    ctxb[0:64, sl, 64 * pr:64 * pr + 64],
                                    qgrot[0:64, b % 2, pr, 64 * (li % 4):64 * (li % 4) + 64])])
                    groups.append([(pdd[64:128, 512 + 64 * pr:512 + 64 * pr + 64],
                                    ctxb[64:128, sl, 64 * pr:64 * pr + 64],
                                    qgrot[64:128, b % 2, pr, 64 * (li % 4):64 * (li % 4) + 64])])
                t_cm = mm(groups, waits=[t_ctx_tok[li], t_qg_res[b],
                                         bankg.get(bka), bankg.get(bkb)])
                t_cm_hist[li] = t_cm
                t_qguse[(hf, b)] = t_cm
                t_ps = vop(lambda eng, sb=li // 2, li=li, u=u: (
                    eng.tensor_add(pz0[:, 0:384], pz0[:, 0:384],
                                   sinr[:, sb % 2, li % 2, :]),
                    eng.tensor_add(pz0[:, 384:390], pz0[:, 384:390],
                                   zst[:, u, :]),
                    eng.drain())[-1],
                    waits=[t_sin_res[li // 2], t_zst[u]])
                t_sinuse[(hf, li // 2)] = t_ps
                if li < 15:
                    t_ctx_tok[li + 1] = emit_ctx(li + 1, [t_cm_hist.get(li - 1)])
                if li % 2 == 1 and li // 2 + 2 < 8:
                    emit_sin_restore(li // 2 + 2)
                if li % 4 == 3 and li // 4 + 2 < 4:
                    emit_qg_restore(li // 4 + 2)
                if dsnap is not None and hf == 0 and li < 4:
                    t_ps = vop(lambda eng, li=li: eng.tensor_copy(
                        dsnap[:, li, 0:390], pz0[:, :]), waits=[t_ps])
                    if li == 3:
                        tap("snap", dsnap[:, :, :], [t_ps])
                t_we = sop(lambda eng, tc2=tc2, li=li, pdd=pdd: (
                    eng.copy(attnw[0:64, tc2 % 2, :, 64 * (li % 2):64 * (li % 2) + 64],
                             pdd[0:64, 0:384].rearrange("p (a b) -> p a b", b=64)),
                    eng.copy(attnw[64:128, tc2 % 2, :, 64 * (li % 2):64 * (li % 2) + 64],
                             pdd[64:128, 512:896].rearrange("p (a b) -> p a b", b=64)))[-1],
                    waits=[t_cm, t_we_prev.get(tc2 % 2)])
                bankg[bka] = t_we
                bankg[bkb] = t_we

                if li % 2 == 1:
                    w = [xch_guard.get(tc2 % 2)]
                    if hf == 0 and tc2 < 2:
                        w.append((None if t_p4_last is None else t_p4_last))
                    dma("sync", xch[:, tc2 % 2, :],
                        x_in[HALO + 128 * tcg:HALO + 128 * tcg + 128, :],
                        waits=w)
                    t_xc = publish("sync")
                    t_wom = []
                    for js in range(2):
                        bk = 4 + js
                        items = []
                        for dc in range(DC):
                            lh = attnloc[:, dc, 128 * tcg:128 * tcg + 128] if dc < 2 \
                                else attnw[:, tc2 % 2, dc - 2, :]
                            items.append((p[bk][:, :], lh,
                                          wo_ap(dc, (512 * js, 512 * js + 512))))
                        items.append((p[bk][:, :], onesrow[:, :],
                                      bo_b[:, 512 * js:512 * js + 512]))
                        t_wm = mm([items], waits=[t_we, t_p4_last, t_wo_tok],
                                  pe_bank=bk)
                        t_wom.append(t_wm)
                    t_womm_last[hf] = t_wom[-1]
                    t_we_prev[tc2 % 2] = t_wom[-1]
                    rs = tc2 % 2
                    t_r1 = vop(lambda eng, rs=rs: (
                        eng.scalar_tensor_tensor(r1b[:, rs, 0:512], p[4][:, :],
                                                 1.0, xch[:, rs, 0:512],
                                                 ALU.mult, ALU.add,
                                                 accum_out=stc[:, 0:1]),
                        eng.scalar_tensor_tensor(r1b[:, rs, 512:1024], p[5][:, :],
                                                 1.0, xch[:, rs, 512:1024],
                                                 ALU.mult, ALU.add,
                                                 accum_out=stc[:, 11:12]),
                        eng.drain())[-1],
                        waits=[t_wom[0], t_wom[1], t_xc, r1_guard.get(rs),
                               t_womm_last.get(hf), stc_guard])
                    bankg[4] = t_r1
                    bankg[5] = t_r1
                    xch_guard[rs] = t_r1
                    t_sq = sop(lambda eng, rs=rs: (
                        eng.activation(sqdump[:, :], r1b[:, rs, 0:512], AF.Square,
                                       accum_out=stc[:, 4:5]),
                        eng.drain(),
                        eng.activation(sqdump[:, :], r1b[:, rs, 512:1024], AF.Square,
                                       accum_out=stc[:, 5:6]))[-1],
                        waits=[t_r1, stc_guard])
                    t_st = vop(lambda eng: (
                        eng.tensor_add(stc[:, 1:2], stc[:, 0:1], stc[:, 11:12]),
                        eng.drain(),
                        eng.tensor_scalar_mul(stc[:, 1:2], stc[:, 1:2], 1.0 / D),
                        eng.drain(),
                        eng.tensor_add(stc[:, 2:3], stc[:, 4:5], stc[:, 5:6]),
                        eng.drain(),
                        eng.tensor_mul(stc[:, 3:4], stc[:, 1:2], stc[:, 1:2]),
                        eng.drain(),
                        eng.scalar_tensor_tensor(stc[:, 3:4], stc[:, 2:3], 1.0 / D,
                                                 stc[:, 3:4], ALU.mult, ALU.subtract),
                        eng.drain())[-1],
                        waits=[t_sq])
                    t_rstd = sop(lambda eng: (
                        eng.activation(stc[:, 6:7], stc[:, 3:4], AF.Ln, bias=epsln[:, :]),
                        eng.drain(),
                        eng.activation(stc[:, 6:7], stc[:, 6:7], AF.Exp, scale=-0.5),
                        eng.drain())[-1],
                        waits=[t_st, t_ms])
                    t_ng = vop(lambda eng: (
                        eng.scalar_tensor_tensor(stc[:, 2:3], stc[:, 1:2], -1.0,
                                                 stc[:, 6:7], ALU.mult, ALU.mult),
                        eng.drain())[-1],
                        waits=[t_rstd])
                    t_nm = sop(lambda eng, rs=rs, tc2=tc2: eng.activation(
                        acc[:, tc2, :], r1b[:, rs, :], AF.Identity,
                        scale=stc[:, 6:7], bias=stc[:, 2:3]),
                        waits=[t_ng])
                    stc_guard = t_nm
                    r1_guard[rs] = t_nm
                    t_norm[(hf, tc2)] = t_nm
                    sl2 = tc2 % 2
                    w2_ = [t_nm, hc_guard.get(sl2)]
                    if hf == 0 and tc2 < 2:
                        w2_.append(t_p4_last)
                    t_hc = vop(lambda eng, sl2=sl2, tc2=tc2:
                               eng.tensor_copy(hcst[:, sl2, :], acc[:, tc2, :]),
                               waits=w2_)
                    ptb2 = pts[sl2]
                    def htr_fn(eng, ptb2=ptb2, sl2=sl2):
                        ins = None
                        for dc2 in range(DC):
                            ins = eng.transpose(ptb2[:, 128 * dc2:128 * dc2 + 128],
                                                hcst[:, sl2, 128 * dc2:128 * dc2 + 128],
                                                idb[:, :])
                        return ins
                    t_tr = S.add("tensor", htr_fn,
                                 waits=[t for t in (t_hc, bankg.get(("pt", sl2))) if t],
                                 inc="sT")
                    hc_guard[sl2] = t_tr
                    def hev_fn(eng, ptb2=ptb2, tc2=tc2):
                        ins = None
                        for dc2 in range(DC):
                            ins = eng.copy(xh[:, dc2, 128 * tc2:128 * tc2 + 128],
                                           ptb2[:, 128 * dc2:128 * dc2 + 128])
                        return ins
                    t_hev = sop(hev_fn, waits=[t_tr, t_p4_last])
                    t_he[(hf, tc2)] = t_hev
                    bankg[("pt", sl2)] = t_hev
                    t_ro = mm([[(p[4][:, 0:8],
                                 xh[:, dc, 128 * tc2:128 * tc2 + 128],
                                 rwq[:, dc, :]) for dc in range(DC)] +
                               [(p[4][:, 0:8], onesrow[:, :], rtb[:, :])]],
                              waits=[t_hev, t_consts], pe_bank=4)
                    t_sm_tok[(hf, tc2)] = sop(
                        lambda eng, tc2=tc2: eng.activation(
                            lgs[:, tc2, :], p[4][:, 0:8], AF.Exp,
                            accum_out=lsum[:, tc2:tc2 + 1]),
                        waits=[t_ro, lg_guard])
                    lg_guard = t_sm_tok[(hf, tc2)]
                    bankg[4] = t_sm_tok[(hf, tc2)]

            if phases < 7 and hf == 0:
                return finish()

            tap("pzend%d" % hf, pz0[:, :], [t_ps])

            for tc2 in range(8):
                tcg = 8 * hf + tc2
                sl = tc2 % 2
                ptb = pts[sl]
                t_tk = vop(lambda eng, tcg=tcg, tc2=tc2: (
                    eng.reciprocal(stc[:, 8:9], lsum[:, tc2:tc2 + 1]),
                    eng.drain(),
                    eng.tensor_scalar_mul(wts[:, tcg, :], lgs[:, tc2, :], stc[:, 8:9]),
                    eng.drain(),
                    eng.reduce_max(stc[:, 9:10], wts[:, tcg, :], axis=AX),
                    eng.drain(),
                    eng.tensor_scalar(lac[:, :], wts[:, tcg, :], stc[:, 9:10], None,
                                      ALU.is_ge),
                    eng.drain(),
                    eng.scalar_tensor_tensor(lac[:, :], lac[:, :], NEG,
                                             wts[:, tcg, :], ALU.mult, ALU.add),
                    eng.drain(),
                    eng.reduce_max(stc[:, 10:11], lac[:, :], axis=AX),
                    eng.drain(),
                    eng.tensor_scalar(lac[:, :], wts[:, tcg, :], stc[:, 10:11], None,
                                      ALU.is_ge),
                    eng.drain(),
                    eng.tensor_mul(wts[:, tcg, :], wts[:, tcg, :], lac[:, :]),
                    eng.drain())[-1],
                    waits=[t_sm_tok[(hf, tc2)], stc_guard])
                stc_guard = t_tk
                t_wc = vop(lambda eng, tcg=tcg:
                           eng.tensor_copy(waugb[:, 0:8], wts[:, tcg, :]),
                           waits=[t_tk, waug_guard, t_ms])
                t_wt = S.add("tensor", lambda eng, ptb=ptb:
                             eng.transpose(ptb[0:9, 0:128], waugb[:, :], idb[:, :]),
                             waits=[t_wc, bankg.get(("pt", sl))], inc="sT")
                waug_guard = t_wt
                t_w9 = sop(lambda eng, ptb=ptb, tc2=tc2:
                           eng.copy(wt9[0:9, tc2, :], ptb[0:9, 0:128]),
                           waits=[t_wt])
                t_we9[(hf, tc2)] = t_w9
                bankg[("pt", sl)] = t_w9
            tap("hT%d" % hf, xh[:, :, 0:TH], [t_he[(hf, 7)]])
            if hf == 0:
                tap("acc0", acc[:, :, :], [t_norm[(0, 7)]])
                tap("wts0", wts[:, 0:8, :], [lg_guard])

            if phases < 8 and hf == 0:
                return finish()

            if hf == 0:
                dma("gpsimd", wstream[:, :, 0:1024],
                    sw1_in[:, 0:1024].rearrange("(c p) j -> p c j", p=128),
                    waits=[t_sz[16]])
                t_sw1lo = publish("gpsimd")
                dma("gpsimd", wstream[:, :, 1024:2048],
                    sw1_in[:, 1024:2048].rearrange("(c p) j -> p c j", p=128),
                    waits=[t_womm_last[0]])
                t_sw1hi = publish("gpsimd")
                t_w1[(0, 0)] = (t_sw1lo, t_sw1hi)
                t_w2[(0, 0)] = (t_sw2, t_sw2)
            else:
                dma("gpsimd", wstream[:, :, 0:1024],
                    sw1_in[:, 0:1024].rearrange("(c p) j -> p c j", p=128),
                    waits=[t_prod_lo[(0, 8)]])
                t_sw1lo = publish("gpsimd")
                dma("gpsimd", wstream[:, :, 1024:2048],
                    sw1_in[:, 1024:2048].rearrange("(c p) j -> p c j", p=128),
                    waits=[t_prod_all[(0, 8)], t_womm_last[1]])
                t_sw1hi = publish("gpsimd")
                dma("gpsimd", slotB[:, :, 0:512],
                    sw2_in[:, 0:512].rearrange("(c p) j -> p c j", p=128),
                    waits=[t_cons_lo[(0, 8)], t_womm_last[1]])
                t_sw2lo = publish("gpsimd")
                dma("gpsimd", slotB[:, :, 512:1024],
                    sw2_in[:, 512:1024].rearrange("(c p) j -> p c j", p=128),
                    waits=[t_cons_all[(0, 8)], t_womm_last[1]])
                t_sw2hi = publish("gpsimd")
                t_w1[(1, 0)] = (t_sw1lo, t_sw1hi)
                t_w2[(1, 0)] = (t_sw2lo, t_sw2hi)

            def emit_tail(tc2):
                nonlocal stc_guard
                tcg = 8 * hf + tc2
                rs = tc2 % 2
                t_ba = None
                for ds in range(2):
                    t_bm = mm([[(p[5][:, :], wt9[0:9, tc2, :],
                                 rb9[0:9, 512 * ds:512 * ds + 512])]],
                              waits=[t_we9[(hf, tc2)], t_consts], pe_bank=5)
                    t_ba = vop(lambda eng, tc2=tc2, ds=ds: (
                               eng.scalar_tensor_tensor(
                                   acc[:, tc2, 512 * ds:512 * ds + 512],
                                   p[5][:, :], 1.0,
                                   acc[:, tc2, 512 * ds:512 * ds + 512],
                                   ALU.mult, ALU.add,
                                   accum_out=lns[:, tc2, ds:ds + 1]),
                               eng.drain())[-1],
                               waits=[t_bm])
                    bankg[5] = t_ba
                t_sq = sop(lambda eng, tc2=tc2: (
                    eng.activation(sqdump[:, :], acc[:, tc2, 0:512], AF.Square,
                                   accum_out=stc[:, 4:5]),
                    eng.drain(),
                    eng.activation(sqdump[:, :], acc[:, tc2, 512:1024], AF.Square,
                                   accum_out=stc[:, 5:6]))[-1],
                    waits=[t_ba, stc_guard])
                t_st = vop(lambda eng, tc2=tc2: (
                    eng.tensor_add(stc[:, 1:2], lns[:, tc2, 0:1], lns[:, tc2, 1:2]),
                    eng.drain(),
                    eng.tensor_scalar_mul(stc[:, 1:2], stc[:, 1:2], 1.0 / D),
                    eng.drain(),
                    eng.tensor_add(stc[:, 2:3], stc[:, 4:5], stc[:, 5:6]),
                    eng.drain(),
                    eng.tensor_mul(stc[:, 3:4], stc[:, 1:2], stc[:, 1:2]),
                    eng.drain(),
                    eng.scalar_tensor_tensor(stc[:, 3:4], stc[:, 2:3], 1.0 / D,
                                             stc[:, 3:4], ALU.mult, ALU.subtract),
                    eng.drain())[-1],
                    waits=[t_sq])
                t_rstd = sop(lambda eng: (
                    eng.activation(stc[:, 6:7], stc[:, 3:4], AF.Ln, bias=epsln[:, :]),
                    eng.drain(),
                    eng.activation(stc[:, 6:7], stc[:, 6:7], AF.Exp, scale=-0.5),
                    eng.drain())[-1],
                    waits=[t_st])
                t_ng = vop(lambda eng: (
                    eng.scalar_tensor_tensor(stc[:, 2:3], stc[:, 1:2], -1.0,
                                             stc[:, 6:7], ALU.mult, ALU.mult),
                    eng.drain())[-1],
                    waits=[t_rstd])
                t_nm = sop(lambda eng, rs=rs, tc2=tc2: eng.activation(
                    r1b[:, rs, :], acc[:, tc2, :], AF.Identity,
                    scale=stc[:, 6:7], bias=stc[:, 2:3]),
                    waits=[t_ng, r1_guard.get(rs)])
                stc_guard = t_nm
                dma("sync",
                    y_out[:, :].rearrange("(c p) j -> p c j", p=128)[:, tcg, :],
                    r1b[:, rs, :], waits=[t_nm])
                t_yd = publish("sync")
                r1_guard[rs] = t_yd
                t_y[(hf, tc2)] = t_yd

            for e in range(NEXP):
                w1lo, w1hi = t_w1[(hf, e)]
                w2lo, w2hi = t_w2[(hf, e)]
                for tt in range(2):
                    t_pf_lo = None
                    t_pf_all = None
                    ge_last = None
                    for f in range(HFC):
                        bk = f % 2
                        wv = [w1lo if f < 8 else w1hi, t_he[(hf, 4 * tt + 3)]]
                        t_pf = mm([[(p[bk][:, :],
                                     wstream[:, dc, 128 * f:128 * f + 128],
                                     xh[:, dc, 512 * tt:512 * tt + 512])
                                    for dc in range(DC)]], waits=wv, pe_bank=bk)
                        if f == 7:
                            t_pf_lo = t_pf
                        if f == HFC - 1:
                            t_pf_all = t_pf
                        wg = [t_pf, t_rb1c, t_h1_guard]
                        if e == 0 and tt == 0:
                            wg += [t_qguse[(hf, 3)], t_sinuse[(hf, 7)],
                                   t_womm_last[hf]]
                        ge_last = sop(lambda eng, bk=bk, f=f, e=e:
                                      eng.activation(h1[:, f, :], p[bk][:, :],
                                                     AF.Gelu,
                                                     bias=rb1c[:, e, f:f + 1]),
                                      waits=wg)
                        bankg[bk] = ge_last
                    if tt == 1:
                        t_prod_lo[(hf, e)] = t_pf_lo
                        t_prod_all[(hf, e)] = t_pf_all
                    cons_last = None
                    cons_lo = None
                    for ds in range(2):
                        for tl in range(4):
                            tc2 = 4 * tt + tl
                            tcg = 8 * hf + tc2
                            bk = 2 + (4 * ds + tl) % 2
                            t_cmm = mm([[(p[bk][:, :], h1[:, f2, 128 * tl:128 * tl + 128],
                                          slotB[:, f2, 512 * ds:512 * ds + 512])
                                         for f2 in range(HFC)]],
                                       waits=[ge_last, w2lo if ds == 0 else w2hi],
                                       pe_bank=bk)
                            t_h1_guard = t_cmm
                            if ds == 0 and tl == 3:
                                cons_lo = t_cmm
                            cons_last = t_cmm
                            wcol = 1.0 if e == 0 else wts[:, tcg, e - 1:e]
                            t_mx = vop(lambda eng, bk=bk, tc2=tc2, ds=ds, wcol=wcol:
                                       eng.scalar_tensor_tensor(
                                           acc[:, tc2, 512 * ds:512 * ds + 512],
                                           p[bk][:, :], wcol,
                                           acc[:, tc2, 512 * ds:512 * ds + 512],
                                           ALU.mult, ALU.add),
                                       waits=[t_cmm])
                            bankg[bk] = t_mx
                    if tt == 1:
                        t_cons_lo[(hf, e)] = cons_lo
                        t_cons_all[(hf, e)] = cons_last
                    if e == NEXP - 1:
                        for tc2_ in range(4 * tt, 4 * tt + 4):
                            emit_tail(tc2_)
                if e < 8:
                    src1 = rw1_in[e, :, :]
                    src2 = rw2_in[e, :, :]
                    dma("gpsimd", wstream[:, :, 0:1024],
                        src1[:, 0:1024].rearrange("(c p) j -> p c j", p=128),
                        waits=[t_prod_lo[(hf, e)]])
                    a = publish("gpsimd")
                    dma("gpsimd", wstream[:, :, 1024:2048],
                        src1[:, 1024:2048].rearrange("(c p) j -> p c j", p=128),
                        waits=[t_prod_all[(hf, e)]])
                    bq = publish("gpsimd")
                    dma("gpsimd", slotB[:, :, 0:512],
                        src2[:, 0:512].rearrange("(c p) j -> p c j", p=128),
                        waits=[t_cons_lo[(hf, e)]])
                    c_ = publish("gpsimd")
                    dma("gpsimd", slotB[:, :, 512:1024],
                        src2[:, 512:1024].rearrange("(c p) j -> p c j", p=128),
                        waits=[t_cons_all[(hf, e)]])
                    d_ = publish("gpsimd")
                    t_w1[(hf, e + 1)] = (a, bq)
                    t_w2[(hf, e + 1)] = (c_, d_)
            if hf == 0:
                dma("gpsimd", slotB[:, 0:8, :],
                    wo_in[:, :].rearrange("(c p) j -> p c j", p=128),
                    waits=[t_cons_all[(0, 8)]])
                t_wo1_tok = publish("gpsimd")

            if phases < 9 and hf == 0:
                return finish()

        return finish()


def host_inputs(inputs, dbg=()):
    x = np.ascontiguousarray(np.asarray(inputs["L0_attn_x"], dtype=np.float32))
    sb1 = np.asarray(inputs["L0_ffn_sb1"], dtype=np.float32)
    rb1 = np.asarray(inputs["L0_ffn_rb1"], dtype=np.float32)
    sb2 = np.asarray(inputs["L0_ffn_sb2"], dtype=np.float32)
    rb2 = np.asarray(inputs["L0_ffn_rb2"], dtype=np.float32)

    ident = np.eye(128, dtype=np.float32)
    ii = np.arange(128)[:, None]
    jj = np.arange(256)[None, :]
    maskr = np.where(jj <= ii + 128, 0.0, NEG).astype(np.float32)
    mask_sf = np.where((jj >= 128) & (jj - 128 <= ii), 0.0, NEG).astype(np.float32)
    maskr = np.tile(maskr, (1, 2))
    mask_sf = np.tile(mask_sf, (1, 2))
    esum = np.zeros((128, NP * 16), np.float32)
    for ci in range(NP):
        for r in range(128):
            g = 2 * ci + r // 64
            esum[r, 16 * ci + g] = 1.0
    ebc = np.zeros((GH, NP * 128), np.float32)
    for ci in range(NP):
        for r in range(128):
            g = 2 * ci + r // 64
            ebc[g, 128 * ci + r] = 1.0
    rb1c = np.zeros((128, NEXP * HFC), np.float32)
    for cc in range(HFC):
        rb1c[:, 0 * HFC + cc] = sb1[128 * cc:128 * cc + 128]
        for e in range(8):
            rb1c[:, (e + 1) * HFC + cc] = rb1[e, 128 * cc:128 * cc + 128]
    rb9 = np.concatenate([rb2, sb2[None, :]], axis=0).astype(np.float32)

    shared = {
        "wq_in": np.ascontiguousarray(inputs["L0_attn_Wq"], dtype=np.float32),
        "wk_in": np.ascontiguousarray(inputs["L0_attn_Wk"], dtype=np.float32),
        "wv_in": np.ascontiguousarray(inputs["L0_attn_Wv"], dtype=np.float32),
        "wo_in": np.ascontiguousarray(inputs["L0_attn_Wo"], dtype=np.float32),
        "bo_in": np.asarray(inputs["L0_attn_bo"], dtype=np.float32).reshape(1, D),
        "rw_in": np.ascontiguousarray(inputs["L0_ffn_router_w"], dtype=np.float32),
        "rtb_in": np.asarray(inputs["L0_ffn_router_b"], dtype=np.float32).reshape(1, 8),
        "sw1_in": np.ascontiguousarray(inputs["L0_ffn_sw1"], dtype=np.float32),
        "sw2_in": np.ascontiguousarray(inputs["L0_ffn_sw2"], dtype=np.float32),
        "rw1_in": np.ascontiguousarray(inputs["L0_ffn_rw1"], dtype=np.float32),
        "rw2_in": np.ascontiguousarray(inputs["L0_ffn_rw2"], dtype=np.float32),
        "rb1c_in": rb1c, "rb9_in": rb9, "ident_in": ident,
        "esum_in": esum, "ebc_in": ebc, "maskr_in": maskr,
    }
    in_maps = []
    for c in range(8):
        sq, qq = c // 4, c % 4
        sl = x[sq, TOK * qq:TOK * qq + TOK]
        halo = (np.zeros((HALO, D), np.float32) if qq == 0
                else x[sq, TOK * qq - HALO:TOK * qq])
        m = dict(shared)
        m["x_in"] = np.ascontiguousarray(np.concatenate([halo, sl], axis=0))
        m["mask0_in"] = mask_sf if qq == 0 else maskr
        pmv = np.zeros((128, 4), np.float32)
        pmv[:, :qq] = 1.0
        m["pm_in"] = pmv
        in_maps.append(m)
    return in_maps


def kernel(**inputs):
    nc = build()
    in_maps = host_inputs(inputs)
    res = run_bass_kernel_spmd(nc, in_maps, core_ids=list(range(8)))
    y = np.stack([np.asarray(res.results[c]["y"]) for c in range(8)])
    return y.reshape(2, 8192, D).astype(np.float32)


# version 21
# speedup vs baseline: 1.0083x; 1.0083x over previous
import contextlib
import numpy as np
import concourse.bass as bass
import concourse.mybir as mybir
from concourse.bass_utils import run_bass_kernel_spmd

F32 = mybir.dt.float32
BF16 = mybir.dt.bfloat16
AF = mybir.ActivationFunctionType
ALU = mybir.AluOpType
AX = mybir.AxisListType.X

D = 1024
DC = 8
TOK = 2048
HALO = 128
XR = TOK + HALO
NT = 16
NU = 32
NBK = 16
NP = 6
GH = 12
HH = 2048
HFC = 16
NEXP = 9
TH = 1024
NEG = -1.0e30

ENGINES = ("sync", "gpsimd", "tensor", "vector", "scalar")


class Sched:
    def __init__(self):
        self.ops = {e: [] for e in ENGINES}
        self.cnt = {}
        self.vc = {e: {} for e in ENGINES}

    def add(self, eng, fn, waits=(), inc=None, amt=1, dma=False):
        w2 = []
        for t in waits:
            if t is None:
                continue
            s, v = t
            if self.vc[eng].get(s, 0) < v:
                w2.append((s, v))
                self.vc[eng][s] = v
        tok = None
        if inc is not None:
            self.cnt[inc] = self.cnt.get(inc, 0) + amt
            tok = (inc, self.cnt[inc])
        self.ops[eng].append((fn, w2, inc, amt, dma))
        return tok


def build(dbg=(), phases=99):
    nc = bass.Bass("TRN2", target_bir_lowering=False,
                   detect_race_conditions=False)

    x_in = nc.dram_tensor("x_in", [XR, D], F32, kind="ExternalInput")
    wq_in = nc.dram_tensor("wq_in", [D, D], F32, kind="ExternalInput")
    wk_in = nc.dram_tensor("wk_in", [D, D], F32, kind="ExternalInput")
    wv_in = nc.dram_tensor("wv_in", [D, D], F32, kind="ExternalInput")
    wo_in = nc.dram_tensor("wo_in", [D, D], F32, kind="ExternalInput")
    bo_in = nc.dram_tensor("bo_in", [1, D], F32, kind="ExternalInput")
    rw_in = nc.dram_tensor("rw_in", [D, 8], F32, kind="ExternalInput")
    rtb_in = nc.dram_tensor("rtb_in", [1, 8], F32, kind="ExternalInput")
    sw1_in = nc.dram_tensor("sw1_in", [D, HH], F32, kind="ExternalInput")
    sw2_in = nc.dram_tensor("sw2_in", [HH, D], F32, kind="ExternalInput")
    rw1_in = nc.dram_tensor("rw1_in", [8, D, HH], F32, kind="ExternalInput")
    rw2_in = nc.dram_tensor("rw2_in", [8, HH, D], F32, kind="ExternalInput")
    rb1c_in = nc.dram_tensor("rb1c_in", [128, NEXP * HFC], F32, kind="ExternalInput")
    rb9_in = nc.dram_tensor("rb9_in", [NEXP, D], F32, kind="ExternalInput")
    ident_in = nc.dram_tensor("ident_in", [128, 128], F32, kind="ExternalInput")
    esum_in = nc.dram_tensor("esum_in", [128, NP * 16], F32, kind="ExternalInput")
    ebc_in = nc.dram_tensor("ebc_in", [GH, NP * 128], F32, kind="ExternalInput")
    maskr_in = nc.dram_tensor("maskr_in", [128, 512], F32, kind="ExternalInput")
    mask0_in = nc.dram_tensor("mask0_in", [128, 512], F32, kind="ExternalInput")
    pm_in = nc.dram_tensor("pm_in", [128, 4], F32, kind="ExternalInput")

    y_out = nc.dram_tensor("y", [TOK, D], F32, kind="ExternalOutput")

    qgd = nc.dram_tensor("qgd", [128, NP, TOK], BF16)
    sd = nc.dram_tensor("sd", [128, NU, 384], BF16)
    ccin = nc.dram_tensor("ccin", [128, 392], F32)
    ccout = nc.dram_tensor("ccout", [512, 392], F32)

    ctx = contextlib.ExitStack()
    with ctx:
        ST = lambda n, s, d: ctx.enter_context(nc.sbuf_tensor(n, s, d))
        PT = lambda n, s, d: ctx.enter_context(nc.psum_tensor(n, s, d))

        wstream = ST("wstream", [128, DC, 2048], BF16)
        slotB = ST("slotB", [128, HFC, D], BF16)
        xh = ST("xh", [128, DC, XR], BF16)
        acc = ST("acc", [128, 8, D], F32)
        attnloc = ST("attnloc", [128, 2, TOK], BF16)
        SCRN = 26688
        scr = ST("scr", [128, SCRN], BF16)
        zst = ST("zst", [128, NU, NP], F32)
        pzsb = ST("pzsb", [128, 392], F32)
        pz0 = ST("pz0", [128, 390], F32)
        wts = ST("wts", [128, NT, 8], F32)
        idb = ST("idb", [128, 128], BF16)
        maskr = ST("maskr", [128, 512], BF16)
        mask0 = ST("mask0", [128, 512], BF16)
        pm = ST("pm", [128, 4], F32)
        esum = ST("esum", [128, NP, 16], BF16)
        ebc = ST("ebc", [GH, NP, 128], F32)
        onesrow = ST("onesrow", [1, 128], BF16)
        onescol = ST("onescol", [128, 1], BF16)
        bo_b = ST("bo_b", [1, D], BF16)
        rb9 = ST("rb9", [NEXP, D], BF16)
        rtb = ST("rtb", [1, 8], BF16)
        rwq = ST("rwq", [128, DC, 8], BF16)
        rb1c = ST("rb1c", [128, NEXP, HFC], F32)
        epsln = ST("epsln", [128, 1], F32)
        waugb = ST("waugb", [128, 9], BF16)
        wt9 = ST("wt9", [9, 8, 128], BF16)
        lg = ST("lg", [128, 8], F32)
        lgs = ST("lgs", [128, 8, 8], F32)
        lsum = ST("lsum", [128, 8], F32)
        stc = ST("stc", [128, 16], F32)
        lac = ST("lac", [128, 8], F32)
        lns = ST("lns", [128, 8, 2], F32)
        rz_t = ST("rz_t", [128, 32], F32)

        def V(off, shape, dt=BF16):
            n = int(np.prod(shape))
            cols = n if dt == BF16 else 2 * n
            assert off + cols <= SCRN, (off, cols)
            ap = scr[:, off:off + cols]
            if dt == F32:
                ap = ap.bitcast(F32)
            if len(shape) > 1:
                names = "abcdef"[:len(shape)]
                ap = ap.rearrange("p (%s) -> p %s" % (" ".join(names), " ".join(names)),
                                  **dict(zip(names, shape)))
            return ap

        R0 = 0
        R1 = 4352
        R2 = 8704
        R3 = 12800
        R4 = 13824
        R5 = 15872
        R6 = 20480
        R7 = 23552

        kloc = V(R0, [2, XR])
        r1b = V(R0, [2, D], F32)
        vloc = V(R1, [17, 256])
        xch = V(R1, [2, D], F32)
        qloc = V(R2, [2, TOK])
        attT = V(R3, [2, 2, 2, 128])
        attb = V(R6, [2, 512])
        xstg = V(R4, [2, D])
        scb = V(R4, [2, 512], F32)
        hcst = V(R4, [2, D])
        qgst = V(R5, [NP, 512])
        qgrot = V(R5, [2, NP, 256])
        sinr = V(R5 + 3072, [2, 2, 384])
        h1 = V(R5, [HFC, 512])
        kgrot = V(R6, [2, 768])
        vgrot = V(R6 + 1536, [2, 768])
        attnw = V(R6, [2, NP, 128])
        ctxb = V(R6 + 1536, [2, 384])
        elust = V(R7, [512], F32)
        sstg = V(R7 + 1024, [4, 384])
        ccst = V(R7, [4, 392], F32)
        rsumb = attnloc[:, 0, 0:1024].bitcast(F32)
        sqdump = V(R2, [512], F32)
        dsnap = V(R7, [4, 392], F32) if "snap" in dbg else None

        pD01 = PT("pd01", [128, 1024], F32)
        pD23 = PT("pd23", [128, 1024], F32)
        p = [pD01[:, 0:512], pD01[:, 512:1024], pD23[:, 0:512], pD23[:, 512:1024],
             PT("ps4", [128, 512], F32), PT("ps5", [128, 512], F32)]
        pts = [PT("pt0", [128, 1024], BF16), PT("pt1", [128, 1024], BF16)]

        sems = {}
        for s in ("sD", "sGd", "sGc", "sT", "sV", "sS", "sC", "sDp", "sGp"):
            sems[s] = ctx.enter_context(nc.semaphore(s))

        S = Sched()
        bankg = {}
        dbg_specs = []
        ndma = {"sync": 0, "gpsimd": 0}
        QSEM = {"sync": "sD", "gpsimd": "sGd"}
        QPUB = {"sync": "sDp", "gpsimd": "sGp"}

        def dma(queue, out_ap, in_ap, waits=()):
            ndma[queue] += 1
            S.add(queue, lambda eng, o=out_ap, i=in_ap: eng.dma_start(o, i),
                  waits=waits, inc=QSEM[queue], amt=16, dma=True)

        def publish(queue):
            return S.add(queue, lambda eng: eng.nop(),
                         waits=[(QSEM[queue], 16 * ndma[queue])], inc=QPUB[queue])

        def mm(groups, waits=(), pe_bank=None):
            w = list(waits)
            if pe_bank is not None:
                g = bankg.get(pe_bank)
                if isinstance(g, list):
                    w.extend(g)
                else:
                    w.append(g)
            def fn(eng, groups=groups):
                ins = None
                for items in groups:
                    n = len(items)
                    for k, (o, l, r) in enumerate(items):
                        ins = eng.matmul(o, l, r, start=(k == 0), stop=(k == n - 1))
                return ins
            return S.add("tensor", fn, waits=w, inc="sT")

        def vop(fn, waits=()):
            return S.add("vector", fn, waits=waits, inc="sV")

        def sop(fn, waits=()):
            return S.add("scalar", fn, waits=waits, inc="sS")

        def gop(fn, waits=()):
            return S.add("gpsimd", fn, waits=waits, inc="sGc")

        def finish():
            S.add("sync", lambda eng: eng.nop(),
                  waits=[("sD", 16 * ndma["sync"]), ("sGd", 16 * ndma["gpsimd"])])
            with nc.Block() as blk:
                for ename in ENGINES:
                    ops = S.ops[ename]
                    drn = ename in ()
                    def body(eng, ops=ops, drn=drn):
                        for fn, waits, inc, amt, isdma in ops:
                            for (s2, v2) in waits:
                                eng.wait_ge(sems[s2], v2)
                            ins = fn(eng)
                            if inc is not None:
                                ins.then_inc(sems[inc], amt)
                            if drn and not isdma:
                                eng.drain()
                    getattr(blk, ename)(body)
            return nc

        def tap(name, ap, waits):
            if name not in dbg:
                return
            if callable(getattr(ap, "ap", None)):
                ap = ap.ap()
            cnts = [cnt for _, cnt in ap.ap]
            total = int(np.prod(cnts[1:])) if len(cnts) > 1 else 1
            o = nc.dram_tensor("dbg_" + name, [cnts[0], total], ap.dtype,
                              kind="ExternalOutput")
            dbg_specs.append(name)
            flat_o = o[:, :]
            if len(cnts) > 2:
                names = "abcdef"[:len(cnts) - 1]
                flat_o = flat_o.rearrange(
                    "p (%s) -> p %s" % (" ".join(names), " ".join(names)),
                    **dict(zip(names, cnts[1:])))
            ndma["sync"] += 1
            S.add("sync", lambda eng, o=flat_o, ap=ap: eng.dma_start(o, ap),
                  waits=waits, inc="sD", amt=16, dma=True)

        dma("gpsimd", idb[:, :], ident_in[:, :])
        t_xevac = {}
        t_xpub = {}
        xstg8 = attnloc[:, :, :].rearrange("p a b -> p (a b)").rearrange(
            "p (s b) -> p s b", s=4)

        def emit_xtr(tc):
            ptb = pts[tc % 2]
            def tr_fn(eng, ptb=ptb, tc=tc):
                ins = None
                for dc2 in range(DC):
                    ins = eng.transpose(ptb[:, 128 * dc2:128 * dc2 + 128],
                                        xstg8[:, tc % 4, 128 * dc2:128 * dc2 + 128],
                                        idb[:, :])
                return ins
            wkey = (tc, "c") if tc < 4 else tc // 4
            t_tr = S.add("tensor", tr_fn,
                         waits=[t for t in (t_xpub[wkey],
                                            bankg.get(("pt", tc % 2))) if t],
                         inc="sT")
            def ev_fn(eng, ptb=ptb, tc=tc):
                ins = None
                for dc2 in range(DC):
                    ins = eng.copy(xh[:, dc2, 128 * tc:128 * tc + 128],
                                   ptb[:, 128 * dc2:128 * dc2 + 128])
                return ins
            t_ev = sop(ev_fn, waits=[t_tr])
            bankg[("pt", tc % 2)] = t_ev
            t_xevac[tc] = t_ev

        for tc in range(4):
            dma("sync", acc[:, tc, :], x_in[128 * tc:128 * tc + 128, :])
        t_x03 = publish("sync")
        for tc in range(4):
            t_xpub[(tc, "c")] = vop(
                lambda eng, tc=tc: eng.tensor_copy(xstg8[:, tc, :], acc[:, tc, :]),
                waits=[t_x03])
        for tc in range(0, 4):
            emit_xtr(tc)
        for tc in range(4, 8):
            dma("gpsimd", xstg8[:, tc % 4, :], x_in[128 * tc:128 * tc + 128, :],
                waits=[t_xevac[tc - 4]])
        t_xpub[1] = publish("gpsimd")
        for tc in range(4, 8):
            emit_xtr(tc)
        for grp in (2, 3, 4):
            lo = 4 * grp
            hi = min(17, lo + 4)
            for tc in range(lo, hi):
                dma("gpsimd", xstg8[:, tc % 4, :],
                    x_in[128 * tc:128 * tc + 128, :], waits=[t_xevac[tc - 4]])
            t_xpub[grp] = publish("gpsimd")
            for tc in range(lo, hi):
                emit_xtr(tc)
        t_xT = t_xevac[16]

        dma("gpsimd", wstream[:, :, 0:1024],
            wq_in[:, :].rearrange("(c p) j -> p c j", p=128))
        dma("gpsimd", wstream[:, :, 1024:2048],
            wk_in[:, :].rearrange("(c p) j -> p c j", p=128))
        t_wq = t_wk = publish("gpsimd")
        for dst, src in [
            (esum[:, :, :], esum_in[:, :].rearrange("p (a b) -> p a b", b=16)),
            (ebc[:, :, :], ebc_in[:, :].rearrange("p (a b) -> p a b", b=128)),
            (maskr[:, :], maskr_in[:, :]),
            (mask0[:, :], mask0_in[:, :]),
            (bo_b[:, :], bo_in[:, :]),
            (rb9[:, :], rb9_in[:, :]),
            (rtb[:, :], rtb_in[:, :]),
            (rwq[:, :, :], rw_in[:, :].rearrange("(c p) j -> p c j", p=128)),
        ]:
            dma("gpsimd", dst, src)
        t_consts = publish("gpsimd")
        dma("sync", pm[:, :], pm_in[:, :])
        dma("sync", rb1c[:, :, :],
            rb1c_in[:, :].rearrange("p (a b) -> p a b", b=HFC))
        t_pm = t_rb1c = publish("sync")
        t_ms = gop(lambda eng: (eng.memset(onesrow[:, :], 1.0),
                                eng.memset(onescol[:, :], 1.0),
                                eng.memset(epsln[:, :], 1e-5),
                                eng.memset(waugb[:, 8:9], 1.0),
                                eng.memset(pzsb[:, :], 0.0),
                                eng.memset(pz0[:, :], 0.0))[-1])

        if phases < 2:
            return finish()

        t_qg_dma = None
        t_qloc_done = None
        t_qmm_last = None
        for tt in range(4):
            c0 = HALO + 512 * tt
            ev = []
            for jc in range(8):
                bk = jc % 2
                t_qmm_last = mm([[(p[bk][:, :],
                                   wstream[:, dc, 128 * jc:128 * jc + 128],
                                   xh[:, dc, c0:c0 + 512]) for dc in range(DC)]],
                                waits=[t_wq, t_xevac[min(16, 4 + 4 * tt)]], pe_bank=bk)
                if jc < 2:
                    t_e = sop(lambda eng, bk=bk, jc=jc, tt=tt:
                              eng.copy(qloc[:, jc, 512 * tt:512 * tt + 512],
                                       p[bk][:, :]),
                              waits=[t_qmm_last])
                    t_qloc_done = t_e
                else:
                    t_e = sop(lambda eng, bk=bk, jc=jc:
                              eng.activation(qgst[:, jc - 2, :], p[bk][:, :], AF.Exp),
                              waits=[t_qmm_last, t_qg_dma])
                bankg[bk] = t_e
                ev.append(t_e)
            t_sum = mm([[(p[2][0:12, :], esum[:, ci, 0:12], qgst[:, ci, :])
                         for ci in range(NP)]], waits=ev[2:] + [t_consts], pe_bank=2)
            t_rec = vop(lambda eng: (eng.reciprocal(rsumb[0:12, :], p[2][0:12, :]),
                                     eng.drain())[-1], waits=[t_sum, t_xT])
            bankg[2] = t_rec
            mul_toks = []
            for ci in range(NP):
                bk = 3 + ci % 2
                t_bc = mm([[(p[bk][:, :], ebc[0:12, ci, :], rsumb[0:12, :])]],
                          waits=[t_rec], pe_bank=bk)
                t_mu = vop(lambda eng, bk=bk, ci=ci:
                           eng.tensor_mul(qgst[:, ci, :], qgst[:, ci, :], p[bk][:, :]),
                           waits=[t_bc])
                bankg[bk] = t_mu
                mul_toks.append(t_mu)
            dma("sync", qgd[:, :, 512 * tt:512 * tt + 512], qgst[:, :, :],
                waits=mul_toks)
            t_qg_dma = publish("sync")
            if tt == 0:
                tap("qgst0", qgst[:, :, :], mul_toks)

        if phases < 2.05:
            return finish()

        kl_slices = [(0, 128)] + [(128 + 512 * k, 512) for k in range(4)]
        ks_i = 0
        t_kloc_done = None
        for jc in range(2):
            for (cs, cn) in kl_slices:
                bk = ks_i % 2
                t_km = mm([[(p[bk][:, 0:cn],
                             wstream[:, dc, 1024 + 128 * jc:1024 + 128 * jc + 128],
                             xh[:, dc, cs:cs + cn]) for dc in range(DC)]],
                          waits=[t_wk, t_xevac[min(16, (cs + cn - 1) // 128)]],
                          pe_bank=bk)
                t_e = sop(lambda eng, bk=bk, jc=jc, cs=cs, cn=cn:
                          eng.copy(kloc[:, jc, cs:cs + cn], p[bk][:, 0:cn]),
                          waits=[t_km])
                bankg[bk] = t_e
                t_kloc_done = t_e
                ks_i += 1
        tap("kloc", kloc, [t_kloc_done])
        tap("qloc", qloc, [t_qloc_done])

        if phases < 2.4:
            return finish()
        dma("gpsimd", wstream[:, :, 0:1024],
            wv_in[:, :].rearrange("(c p) j -> p c j", p=128),
            waits=[t_qmm_last])
        t_wv = publish("gpsimd")
        dma("gpsimd", slotB[:, :, :],
            sw2_in[:, :].rearrange("(c p) j -> p c j", p=128))
        t_sw2 = publish("gpsimd")

        t_hmv = mm([[(p[2][:, :], xh[:, dc, 0:128], wstream[:, dc, 0:512])
                     for dc in range(DC)]], waits=[t_wv, t_xevac[0]], pe_bank=2)
        t_ev0 = sop(lambda eng: eng.copy(vloc[:, 0, :], p[2][:, 0:256]),
                    waits=[t_hmv])
        bankg[2] = t_ev0
        t_vloc = {0: t_ev0}

        if phases < 2.35:
            return finish()
        t_pz = None
        t_sz = {}
        t_sd_dma = {}
        sstg_tok = {}
        t_zst = {}
        t_elu_guard = None
        import os as _os
        _ntc = int(_os.environ.get("KNTC", "17"))
        for tc in range(1, min(_ntc, 17) if phases >= 2.38 else 2):
            tci = tc - 1
            xc0 = 128 * tc
            w_rot = [t_sz.get(tc - 2)]
            t_vg = []
            for js in range(2):
                bk = 2 + js
                t_vm = mm([[(p[bk][:, :], xh[:, dc, xc0:xc0 + 128],
                             wstream[:, dc, 512 * js:512 * js + 512])
                            for dc in range(DC)]],
                          waits=[t_wv, t_xevac[tc]], pe_bank=bk)
                if js == 0:
                    t_e2 = vop(lambda eng, bk=bk, tc=tc: (
                        eng.tensor_copy(vloc[:, tc, :], p[bk][:, 0:256]),
                        eng.tensor_copy(vgrot[:, tc % 2, 0:256],
                                        p[bk][:, 256:512]))[-1],
                        waits=[t_vm] + w_rot)
                    bankg[bk] = t_e2
                    t_vloc[tc] = t_e2
                    t_vg.append(t_e2)
                else:
                    t_e2 = vop(lambda eng, bk=bk, tc=tc:
                               eng.tensor_copy(vgrot[:, tc % 2, 256:768], p[bk][:, :]),
                               waits=[t_vm] + w_rot)
                    bankg[bk] = t_e2
                    t_vg.append(t_e2)
            if phases < 2.6:
                continue
            t_kg = []
            for js, (j0, jn, g0) in enumerate([(256, 512, 0), (768, 256, 512)]):
                bk = 4 + js
                t_kmm = mm([[(p[bk][:, 0:jn], xh[:, dc, xc0:xc0 + 128],
                              wstream[:, dc, 1024 + j0:1024 + j0 + jn])
                             for dc in range(DC)]],
                           waits=[t_wk, t_xevac[tc]], pe_bank=bk)
                t_m1 = vop(lambda eng, bk=bk, jn=jn:
                           eng.tensor_scalar_min(elust[:, 0:jn], p[bk][:, 0:jn], 0.0),
                           waits=[t_kmm, t_elu_guard])
                t_m2 = sop(lambda eng, jn=jn:
                           eng.activation(elust[:, 0:jn], elust[:, 0:jn], AF.Exp),
                           waits=[t_m1])
                t_m3 = vop(lambda eng, bk=bk, jn=jn, g0=g0, tc=tc:
                           eng.scalar_tensor_tensor(kgrot[:, tc % 2, g0:g0 + jn],
                                                    p[bk][:, 0:jn], 0.0,
                                                    elust[:, 0:jn], ALU.max, ALU.add),
                           waits=[t_m2] + w_rot)
                t_elu_guard = t_m3
                bankg[bk] = t_m3
                t_kg.append(t_m3)
            if phases < 2.8:
                continue
            for uu in range(2):
                u = 2 * tci + uu
                bk = uu
                rows = slice(64 * uu, 64 * uu + 64)
                groups = []
                for pr in range(NP):
                    ke = kgrot[rows, tc % 2, 128 * pr:128 * pr + 64]
                    ko = kgrot[rows, tc % 2, 128 * pr + 64:128 * pr + 128]
                    ve = vgrot[rows, tc % 2, 128 * pr:128 * pr + 64]
                    vo = vgrot[rows, tc % 2, 128 * pr + 64:128 * pr + 128]
                    groups.append([(p[bk][0:64, 64 * pr:64 * pr + 64], ke, ve)])
                    groups.append([(p[bk][0:64, 384 + pr:385 + pr], ke,
                                    onescol[rows, :])])
                    groups.append([(p[bk][64:128, 64 * pr:64 * pr + 64], ko, vo)])
                    groups.append([(p[bk][64:128, 384 + pr:385 + pr], ko,
                                    onescol[rows, :])])
                t_smm = mm(groups, waits=t_vg + t_kg + [t_ms], pe_bank=bk)
                t_sz[tc] = t_smm
                t_pz = vop(lambda eng, bk=bk:
                           eng.tensor_add(pzsb[:, 0:390], pzsb[:, 0:390], p[bk][:, 0:390]),
                           waits=[t_smm])
                t_cp = vop(lambda eng, bk=bk, u=u: (
                    eng.tensor_copy(sstg[:, u % 4, :], p[bk][:, 0:384]),
                    eng.tensor_copy(zst[:, u, :], p[bk][:, 384:390]))[-1],
                    waits=[t_smm, t_sd_dma.get(u - 4)])
                t_zst[u] = t_cp
                bankg[bk] = t_cp
                sstg_tok[u] = t_cp
                if u % 4 == 3 and phases >= 2.9:
                    dma("sync", sd[:, u - 3:u + 1, :], sstg[:, :, :],
                        waits=[sstg_tok[u - k] for k in range(4)] + [t_zst[u]])
                    t_d = publish("sync")
                    for k in range(4):
                        t_sd_dma[u - 3 + k] = t_d
        tap("pzsb", pzsb[:, 0:390], [t_pz])
        tap("sdd", sd[:, :, :], [t_sd_dma[NU - 1]])
        tap("qgdd", qgd[:, :, :], [t_qg_dma])
        tap("vloc", vloc, [t_vloc[16]])

        if phases < 4:
            return finish()

        dma("sync", ccin[:, :], pzsb[:, :], waits=[t_pz])
        t_ccin = publish("sync")
        t_cc = S.add("gpsimd",
                     lambda eng: eng.collective_compute(
                         "AllGather", ALU.bypass,
                         replica_groups=[[0, 1, 2, 3], [4, 5, 6, 7]],
                         ins=[ccin.ap().opt()], outs=[ccout.ap().opt()]),
                     waits=[t_ccin], inc="sC", amt=1, dma=True)
        dma("gpsimd", ccst[:, :, :],
            ccout[:, :].rearrange("(r p) f -> p r f", p=128),
            waits=[t_cc, t_sd_dma[NU - 1], t_elu_guard])
        t_ccb = publish("gpsimd")

        dma("gpsimd", wstream[:, :, 1024:2048],
            wo_in[:, :].rearrange("(c p) j -> p c j", p=128),
            waits=[t_sz[16]])
        t_wo0 = publish("gpsimd")

        if phases < 5:
            return finish()

        att_tok = {}
        t_p4_last = None
        pair_state = {}

        def p4_front(idx):
            i, hp = idx // 2, idx % 2
            sl = idx % 2
            pdd = pD23 if sl else pD01
            bka, bkb = (2, 3) if sl else (0, 1)
            t_s = mm([[(pdd[:, 0:256],
                        qloc[0:64, hp, 128 * i:128 * i + 128],
                        kloc[0:64, hp, 128 * i:128 * i + 256])],
                      [(pdd[:, 512:768],
                        qloc[64:128, hp, 128 * i:128 * i + 128],
                        kloc[64:128, hp, 128 * i:128 * i + 256])]],
                     waits=[t_qloc_done, t_kloc_done, bankg.get(bka),
                            bankg.get(bkb)])
            msk = mask0 if i == 0 else maskr
            t_a = vop(lambda eng, pdd=pdd, sl=sl, msk=msk:
                      eng.tensor_add(
                          scb[:, sl, :].rearrange("p (a b) -> p a b", b=256),
                          pdd[:, :].rearrange("p (a b) -> p a b", b=512)[:, :, 0:256],
                          msk[:, :].rearrange("p (a b) -> p a b", b=256)),
                      waits=[t_s, att_tok.get(("sc", sl)), t_consts])
            bankg[bka] = t_a
            bankg[bkb] = t_a
            t_x = sop(lambda eng, sl=sl: (
                eng.activation(scb[:, sl, 0:256], scb[:, sl, 0:256], AF.Exp,
                               scale=0.125, accum_out=lac[:, 4 * sl:4 * sl + 1]),
                eng.drain(),
                eng.activation(scb[:, sl, 256:512], scb[:, sl, 256:512], AF.Exp,
                               scale=0.125,
                               accum_out=lac[:, 4 * sl + 1:4 * sl + 2]))[-1],
                waits=[t_a, att_tok.get(("lac", sl))])
            pair_state[idx] = t_x

        def p4_back(idx):
            nonlocal t_p4_last
            i, hp = idx // 2, idx % 2
            sl = idx % 2
            t_x = pair_state.pop(idx)
            t_r = vop(lambda eng, sl=sl: (
                eng.reciprocal(lac[:, 4 * sl + 2:4 * sl + 4],
                               lac[:, 4 * sl:4 * sl + 2]),
                eng.drain(),
                eng.tensor_tensor(
                    attb[:, sl, :].rearrange("p (a b) -> p a b", b=256),
                    scb[:, sl, :].rearrange("p (a b) -> p a b", b=256),
                    bass.AP(lac, 4 * sl + 2, [[8, 128], [1, 2], [0, 256]]),
                    ALU.mult))[-1],
                waits=[t_x, att_tok.get(("attb", sl)), t_sz[16]])
            att_tok[("sc", sl)] = t_r
            att_tok[("lac", sl)] = t_r
            ptb = pts[sl]
            t_t = S.add("tensor", lambda eng, ptb=ptb, sl=sl: [
                eng.transpose(ptb[:, 128 * q2:128 * q2 + 128],
                              attb[:, sl, 128 * q2:128 * q2 + 128], idb[:, :])
                for q2 in range(4)][-1],
                waits=[t for t in (t_r, bankg.get(("pt", sl))) if t], inc="sT")
            att_tok[("attb", sl)] = t_t
            t_te = sop(lambda eng, ptb=ptb, sl=sl: [
                eng.copy(attT[:, sl, q2 // 2, q2 % 2, :],
                         ptb[:, 128 * q2:128 * q2 + 128])
                for q2 in range(4)][-1],
                waits=[t_t, att_tok.get(("attT", sl))])
            bankg[("pt", sl)] = t_te
            bk2 = 4 + sl
            t_av = mm([[(p[bk2][0:64, 0:128],
                         vloc[:, i + kc, 128 * hp:128 * hp + 64],
                         attT[:, sl, 0, kc, :]) for kc in range(2)],
                       [(p[bk2][64:128, 0:128],
                         vloc[:, i + kc, 128 * hp + 64:128 * hp + 128],
                         attT[:, sl, 1, kc, :]) for kc in range(2)]],
                      waits=[t_te, t_vloc[i], t_vloc[i + 1]], pe_bank=bk2)
            att_tok[("attT", sl)] = t_av
            t_ae = sop(lambda eng, bk2=bk2, hp=hp, i=i:
                       eng.copy(attnloc[:, hp, 128 * i:128 * i + 128],
                                p[bk2][:, 0:128]),
                       waits=[t_av])
            bankg[bk2] = t_ae
            t_p4_last = t_ae

        for k in range(2 * NBK + 1):
            if k < 2 * NBK:
                p4_front(k)
            if k >= 1:
                p4_back(k - 1)
        tap("attnloc", attnloc, [t_p4_last])

        t_pfx = None
        for r in range(4):
            t_pfx = vop(lambda eng, r=r: (
                        eng.scalar_tensor_tensor(pz0[:, :], ccst[:, r, 0:390],
                                                 pm[:, r:r + 1], pz0[:, :],
                                                 ALU.mult, ALU.add),
                        eng.drain())[-1],
                        waits=[t_ccb, t_ms, t_pm])
        tap("pz0", pz0[:, :], [t_pfx])

        if phases < 6:
            return finish()

        rz_bcast = [bass.AP(rz_t, 8, [[32, 128], [1, NP], [0, 64]]),
                    bass.AP(rz_t, 24, [[32, 128], [1, NP], [0, 64]])]
        t_womm_last = {}
        t_prod_lo = {}
        t_prod_all = {}
        t_cons_lo = {}
        t_cons_all = {}
        t_he = {}
        t_sm_tok = {}
        t_we9 = {}
        t_y = {}
        r1_guard = {}
        xch_guard = {}
        stc_guard = None
        lg_guard = None
        waug_guard = None
        hc_guard = {}
        t_norm = {}
        t_h1_guard = None
        t_qguse = {}
        t_sinuse = {}
        t_w1 = {}
        t_w2 = {}
        t_wo1_tok = None

        for hf in range(2):
            t_qg_res = {}
            t_sin_res = {}

            def emit_restores(b, w_extra=()):
                u0 = 16 * hf + 4 * b
                w = [t_qg_dma, t_qguse.get((hf, b - 2))] + list(w_extra)
                dma("sync", qgrot[:, b % 2, :, :],
                    qgd[:, :, 64 * u0:64 * u0 + 256], waits=w)
                for sb in (2 * b, 2 * b + 1):
                    u1 = 16 * hf + 2 * sb
                    w2 = [t_sd_dma[min(NU - 1, u1 + 1)],
                          t_sinuse.get((hf, sb - 2))] + list(w_extra)
                    dma("sync", sinr[:, sb % 2, :, :],
                        sd[:, u1:u1 + 2, :], waits=w2)
                t_qg_res[b] = t_sin_res[2 * b] = t_sin_res[2 * b + 1] = publish("sync")

            def emit_qg_restore(b, w_extra=()):
                u0 = 16 * hf + 4 * b
                w = [t_qg_dma, t_qguse.get((hf, b - 2))] + list(w_extra)
                dma("sync", qgrot[:, b % 2, :, :],
                    qgd[:, :, 64 * u0:64 * u0 + 256], waits=w)
                t_qg_res[b] = publish("sync")

            def emit_sin_restore(sb, w_extra=()):
                u1 = 16 * hf + 2 * sb
                w2 = [t_sd_dma[min(NU - 1, u1 + 1)],
                      t_sinuse.get((hf, sb - 2))] + list(w_extra)
                dma("sync", sinr[:, sb % 2, :, :],
                    sd[:, u1:u1 + 2, :], waits=w2)
                t_sin_res[sb] = publish("sync")

            w_extra = [t_cons_all.get((0, 8))] if hf == 1 else []
            for b_ in range(2):
                u0 = 16 * hf + 4 * b_
                dma("sync", qgrot[:, b_ % 2, :, :],
                    qgd[:, :, 64 * u0:64 * u0 + 256],
                    waits=[t_qg_dma] + w_extra)
            for sb_ in range(2):
                u1 = 16 * hf + 2 * sb_
                dma("sync", sinr[:, sb_ % 2, :, :], sd[:, u1:u1 + 2, :],
                    waits=[t_sd_dma[min(NU - 1, u1 + 1)]] + w_extra)
            t_qg_res[0] = t_qg_res[1] = t_sin_res[0] = t_sin_res[1] =                 publish("sync")

            wo_ap = (lambda dc, c: wstream[:, dc, 1024 + c[0]:1024 + c[1]]) if hf == 0 \
                else (lambda dc, c: slotB[:, dc, c[0]:c[1]])
            t_wo_tok = t_wo0 if hf == 0 else t_wo1_tok

            t_we_prev = {}
            t_cm_hist = {}

            def emit_htr_router(tc2):
                nonlocal lg_guard
                sl2 = tc2 % 2
                w2_ = [t_norm[(hf, tc2)], hc_guard.get(sl2)]
                if hf == 0 and tc2 < 2:
                    w2_.append(t_p4_last)
                t_hc = vop(lambda eng, sl2=sl2, tc2=tc2:
                           eng.tensor_copy(hcst[:, sl2, :], acc[:, tc2, :]),
                           waits=w2_)
                ptb2 = pts[sl2]
                def htr_fn(eng, ptb2=ptb2, sl2=sl2):
                    ins = None
                    for dc2 in range(DC):
                        ins = eng.transpose(ptb2[:, 128 * dc2:128 * dc2 + 128],
                                            hcst[:, sl2, 128 * dc2:128 * dc2 + 128],
                                            idb[:, :])
                    return ins
                t_tr = S.add("tensor", htr_fn,
                             waits=[t for t in (t_hc, bankg.get(("pt", sl2))) if t],
                             inc="sT")
                hc_guard[sl2] = t_tr
                def hev_fn(eng, ptb2=ptb2, tc2=tc2):
                    ins = None
                    for dc2 in range(DC):
                        ins = eng.copy(xh[:, dc2, 128 * tc2:128 * tc2 + 128],
                                       ptb2[:, 128 * dc2:128 * dc2 + 128])
                    return ins
                t_hev = sop(hev_fn, waits=[t_tr, t_p4_last])
                t_he[(hf, tc2)] = t_hev
                bankg[("pt", sl2)] = t_hev
                t_ro = mm([[(p[4][:, 0:8],
                             xh[:, dc, 128 * tc2:128 * tc2 + 128],
                             rwq[:, dc, :]) for dc in range(DC)] +
                           [(p[4][:, 0:8], onesrow[:, :], rtb[:, :])]],
                          waits=[t_hev, t_consts], pe_bank=4)
                t_sm_tok[(hf, tc2)] = sop(
                    lambda eng, tc2=tc2: eng.activation(
                        lgs[:, tc2, :], p[4][:, 0:8], AF.Exp,
                        accum_out=lsum[:, tc2:tc2 + 1]),
                    waits=[t_ro, lg_guard])
                lg_guard = t_sm_tok[(hf, tc2)]
                bankg[4] = t_sm_tok[(hf, tc2)]

            def emit_ctx(li2, waits):
                sl = li2 % 2
                return vop(lambda eng, sl=sl: (
                    eng.tensor_scalar_add(rz_t[:, 16 * sl:16 * sl + 6],
                                          pz0[:, 384:390], 1e-6),
                    eng.drain(),
                    eng.reciprocal(rz_t[:, 16 * sl + 8:16 * sl + 14],
                                   rz_t[:, 16 * sl:16 * sl + 6]),
                    eng.drain(),
                    eng.tensor_tensor(
                        ctxb[:, sl, :].rearrange("p (a b) -> p a b", b=64),
                        pz0[:, 0:384].rearrange("p (a b) -> p a b", b=64),
                        rz_bcast[sl], ALU.mult))[-1],
                    waits=waits)

            w = [t_pfx, t_sz[16]] if hf == 0 else [t_cons_all[(0, 8)]]
            t_ctx_tok = {0: emit_ctx(0, w)}
            for li in range(16):
                u = 16 * hf + li
                tc2 = li // 2
                tcg = 8 * hf + tc2
                b = li // 4
                sl = li % 2
                pdd = pD01 if li % 2 == 0 else pD23
                bka, bkb = (0, 1) if li % 2 == 0 else (2, 3)
                groups = []
                for pr in range(NP):
                    groups.append([(pdd[0:64, 64 * pr:64 * pr + 64],
                                    ctxb[0:64, sl, 64 * pr:64 * pr + 64],
                                    qgrot[0:64, b % 2, pr, 64 * (li % 4):64 * (li % 4) + 64])])
                    groups.append([(pdd[64:128, 512 + 64 * pr:512 + 64 * pr + 64],
                                    ctxb[64:128, sl, 64 * pr:64 * pr + 64],
                                    qgrot[64:128, b % 2, pr, 64 * (li % 4):64 * (li % 4) + 64])])
                t_cm = mm(groups, waits=[t_ctx_tok[li], t_qg_res[b],
                                         bankg.get(bka), bankg.get(bkb)])
                t_cm_hist[li] = t_cm
                t_qguse[(hf, b)] = t_cm
                t_ps = vop(lambda eng, sb=li // 2, li=li, u=u: (
                    eng.tensor_add(pz0[:, 0:384], pz0[:, 0:384],
                                   sinr[:, sb % 2, li % 2, :]),
                    eng.tensor_add(pz0[:, 384:390], pz0[:, 384:390],
                                   zst[:, u, :]),
                    eng.drain())[-1],
                    waits=[t_sin_res[li // 2], t_zst[u]])
                t_sinuse[(hf, li // 2)] = t_ps
                if li < 15:
                    t_ctx_tok[li + 1] = emit_ctx(li + 1, [t_cm_hist.get(li - 1)])
                if li % 2 == 1 and li // 2 + 2 < 8:
                    emit_sin_restore(li // 2 + 2)
                if li % 4 == 3 and li // 4 + 2 < 4:
                    emit_qg_restore(li // 4 + 2)
                if dsnap is not None and hf == 0 and li < 4:
                    t_ps = vop(lambda eng, li=li: eng.tensor_copy(
                        dsnap[:, li, 0:390], pz0[:, :]), waits=[t_ps])
                    if li == 3:
                        tap("snap", dsnap[:, :, :], [t_ps])
                t_we = sop(lambda eng, tc2=tc2, li=li, pdd=pdd: (
                    eng.copy(attnw[0:64, tc2 % 2, :, 64 * (li % 2):64 * (li % 2) + 64],
                             pdd[0:64, 0:384].rearrange("p (a b) -> p a b", b=64)),
                    eng.copy(attnw[64:128, tc2 % 2, :, 64 * (li % 2):64 * (li % 2) + 64],
                             pdd[64:128, 512:896].rearrange("p (a b) -> p a b", b=64)))[-1],
                    waits=[t_cm, t_we_prev.get(tc2 % 2)])
                bankg[bka] = t_we
                bankg[bkb] = t_we

                if li % 2 == 1:
                    if tc2 >= 1:
                        emit_htr_router(tc2 - 1)
                    w = [xch_guard.get(tc2 % 2)]
                    if hf == 0 and tc2 < 2:
                        w.append((None if t_p4_last is None else t_p4_last))
                    dma("sync", xch[:, tc2 % 2, :],
                        x_in[HALO + 128 * tcg:HALO + 128 * tcg + 128, :],
                        waits=w)
                    t_xc = publish("sync")
                    t_wom = []
                    for js in range(2):
                        bk = 4 + js
                        items = []
                        for dc in range(DC):
                            lh = attnloc[:, dc, 128 * tcg:128 * tcg + 128] if dc < 2 \
                                else attnw[:, tc2 % 2, dc - 2, :]
                            items.append((p[bk][:, :], lh,
                                          wo_ap(dc, (512 * js, 512 * js + 512))))
                        items.append((p[bk][:, :], onesrow[:, :],
                                      bo_b[:, 512 * js:512 * js + 512]))
                        t_wm = mm([items], waits=[t_we, t_p4_last, t_wo_tok],
                                  pe_bank=bk)
                        t_wom.append(t_wm)
                    t_womm_last[hf] = t_wom[-1]
                    t_we_prev[tc2 % 2] = t_wom[-1]
                    rs = tc2 % 2
                    t_r1 = vop(lambda eng, rs=rs: (
                        eng.scalar_tensor_tensor(r1b[:, rs, 0:512], p[4][:, :],
                                                 1.0, xch[:, rs, 0:512],
                                                 ALU.mult, ALU.add,
                                                 accum_out=stc[:, 0:1]),
                        eng.scalar_tensor_tensor(r1b[:, rs, 512:1024], p[5][:, :],
                                                 1.0, xch[:, rs, 512:1024],
                                                 ALU.mult, ALU.add,
                                                 accum_out=stc[:, 11:12]),
                        eng.drain())[-1],
                        waits=[t_wom[0], t_wom[1], t_xc, r1_guard.get(rs),
                               t_womm_last.get(hf), stc_guard])
                    bankg[4] = t_r1
                    bankg[5] = t_r1
                    xch_guard[rs] = t_r1
                    t_sq = sop(lambda eng, rs=rs: (
                        eng.activation(sqdump[:, :], r1b[:, rs, 0:512], AF.Square,
                                       accum_out=stc[:, 4:5]),
                        eng.drain(),
                        eng.activation(sqdump[:, :], r1b[:, rs, 512:1024], AF.Square,
                                       accum_out=stc[:, 5:6]))[-1],
                        waits=[t_r1, stc_guard])
                    t_st = vop(lambda eng: (
                        eng.tensor_add(stc[:, 1:2], stc[:, 0:1], stc[:, 11:12]),
                        eng.drain(),
                        eng.tensor_scalar_mul(stc[:, 1:2], stc[:, 1:2], 1.0 / D),
                        eng.drain(),
                        eng.tensor_add(stc[:, 2:3], stc[:, 4:5], stc[:, 5:6]),
                        eng.drain(),
                        eng.tensor_mul(stc[:, 3:4], stc[:, 1:2], stc[:, 1:2]),
                        eng.drain(),
                        eng.scalar_tensor_tensor(stc[:, 3:4], stc[:, 2:3], 1.0 / D,
                                                 stc[:, 3:4], ALU.mult, ALU.subtract),
                        eng.drain())[-1],
                        waits=[t_sq])
                    t_rstd = sop(lambda eng: (
                        eng.activation(stc[:, 6:7], stc[:, 3:4], AF.Ln, bias=epsln[:, :]),
                        eng.drain(),
                        eng.activation(stc[:, 6:7], stc[:, 6:7], AF.Exp, scale=-0.5),
                        eng.drain())[-1],
                        waits=[t_st, t_ms])
                    t_ng = vop(lambda eng: (
                        eng.scalar_tensor_tensor(stc[:, 2:3], stc[:, 1:2], -1.0,
                                                 stc[:, 6:7], ALU.mult, ALU.mult),
                        eng.drain())[-1],
                        waits=[t_rstd])
                    t_nm = sop(lambda eng, rs=rs, tc2=tc2: eng.activation(
                        acc[:, tc2, :], r1b[:, rs, :], AF.Identity,
                        scale=stc[:, 6:7], bias=stc[:, 2:3]),
                        waits=[t_ng])
                    stc_guard = t_nm
                    r1_guard[rs] = t_nm
                    t_norm[(hf, tc2)] = t_nm

            emit_htr_router(7)

            if phases < 7 and hf == 0:
                return finish()

            tap("pzend%d" % hf, pz0[:, :], [t_ps])

            for tc2 in range(8):
                tcg = 8 * hf + tc2
                sl = tc2 % 2
                ptb = pts[sl]
                t_tk = vop(lambda eng, tcg=tcg, tc2=tc2: (
                    eng.reciprocal(stc[:, 8:9], lsum[:, tc2:tc2 + 1]),
                    eng.drain(),
                    eng.tensor_scalar_mul(wts[:, tcg, :], lgs[:, tc2, :], stc[:, 8:9]),
                    eng.drain(),
                    eng.reduce_max(stc[:, 9:10], wts[:, tcg, :], axis=AX),
                    eng.drain(),
                    eng.tensor_scalar(lac[:, :], wts[:, tcg, :], stc[:, 9:10], None,
                                      ALU.is_ge),
                    eng.drain(),
                    eng.scalar_tensor_tensor(lac[:, :], lac[:, :], NEG,
                                             wts[:, tcg, :], ALU.mult, ALU.add),
                    eng.drain(),
                    eng.reduce_max(stc[:, 10:11], lac[:, :], axis=AX),
                    eng.drain(),
                    eng.tensor_scalar(lac[:, :], wts[:, tcg, :], stc[:, 10:11], None,
                                      ALU.is_ge),
                    eng.drain(),
                    eng.tensor_mul(wts[:, tcg, :], wts[:, tcg, :], lac[:, :]),
                    eng.drain())[-1],
                    waits=[t_sm_tok[(hf, tc2)], stc_guard])
                stc_guard = t_tk
                t_wc = vop(lambda eng, tcg=tcg:
                           eng.tensor_copy(waugb[:, 0:8], wts[:, tcg, :]),
                           waits=[t_tk, waug_guard, t_ms])
                t_wt = S.add("tensor", lambda eng, ptb=ptb:
                             eng.transpose(ptb[0:9, 0:128], waugb[:, :], idb[:, :]),
                             waits=[t_wc, bankg.get(("pt", sl))], inc="sT")
                waug_guard = t_wt
                t_w9 = sop(lambda eng, ptb=ptb, tc2=tc2:
                           eng.copy(wt9[0:9, tc2, :], ptb[0:9, 0:128]),
                           waits=[t_wt])
                t_we9[(hf, tc2)] = t_w9
                bankg[("pt", sl)] = t_w9
            tap("hT%d" % hf, xh[:, :, 0:TH], [t_he[(hf, 7)]])
            if hf == 0:
                tap("acc0", acc[:, :, :], [t_norm[(0, 7)]])
                tap("wts0", wts[:, 0:8, :], [lg_guard])

            if phases < 8 and hf == 0:
                return finish()

            if hf == 0:
                dma("gpsimd", wstream[:, :, 0:1024],
                    sw1_in[:, 0:1024].rearrange("(c p) j -> p c j", p=128),
                    waits=[t_sz[16]])
                t_sw1lo = publish("gpsimd")
                dma("gpsimd", wstream[:, :, 1024:2048],
                    sw1_in[:, 1024:2048].rearrange("(c p) j -> p c j", p=128),
                    waits=[t_womm_last[0]])
                t_sw1hi = publish("gpsimd")
                t_w1[(0, 0)] = (t_sw1lo, t_sw1hi)
                t_w2[(0, 0)] = (t_sw2, t_sw2)
            else:
                dma("gpsimd", wstream[:, :, 0:1024],
                    sw1_in[:, 0:1024].rearrange("(c p) j -> p c j", p=128),
                    waits=[t_prod_lo[(0, 8)]])
                t_sw1lo = publish("gpsimd")
                dma("gpsimd", wstream[:, :, 1024:2048],
                    sw1_in[:, 1024:2048].rearrange("(c p) j -> p c j", p=128),
                    waits=[t_prod_all[(0, 8)], t_womm_last[1]])
                t_sw1hi = publish("gpsimd")
                dma("gpsimd", slotB[:, :, 0:512],
                    sw2_in[:, 0:512].rearrange("(c p) j -> p c j", p=128),
                    waits=[t_cons_lo[(0, 8)], t_womm_last[1]])
                t_sw2lo = publish("gpsimd")
                dma("gpsimd", slotB[:, :, 512:1024],
                    sw2_in[:, 512:1024].rearrange("(c p) j -> p c j", p=128),
                    waits=[t_cons_all[(0, 8)], t_womm_last[1]])
                t_sw2hi = publish("gpsimd")
                t_w1[(1, 0)] = (t_sw1lo, t_sw1hi)
                t_w2[(1, 0)] = (t_sw2lo, t_sw2hi)

            def emit_tail(tc2):
                nonlocal stc_guard
                tcg = 8 * hf + tc2
                rs = tc2 % 2
                t_ba = None
                for ds in range(2):
                    t_bm = mm([[(p[5][:, :], wt9[0:9, tc2, :],
                                 rb9[0:9, 512 * ds:512 * ds + 512])]],
                              waits=[t_we9[(hf, tc2)], t_consts], pe_bank=5)
                    t_ba = vop(lambda eng, tc2=tc2, ds=ds: (
                               eng.scalar_tensor_tensor(
                                   acc[:, tc2, 512 * ds:512 * ds + 512],
                                   p[5][:, :], 1.0,
                                   acc[:, tc2, 512 * ds:512 * ds + 512],
                                   ALU.mult, ALU.add,
                                   accum_out=lns[:, tc2, ds:ds + 1]),
                               eng.drain())[-1],
                               waits=[t_bm])
                    bankg[5] = t_ba
                t_sq = sop(lambda eng, tc2=tc2: (
                    eng.activation(sqdump[:, :], acc[:, tc2, 0:512], AF.Square,
                                   accum_out=stc[:, 4:5]),
                    eng.drain(),
                    eng.activation(sqdump[:, :], acc[:, tc2, 512:1024], AF.Square,
                                   accum_out=stc[:, 5:6]))[-1],
                    waits=[t_ba, stc_guard])
                t_st = vop(lambda eng, tc2=tc2: (
                    eng.tensor_add(stc[:, 1:2], lns[:, tc2, 0:1], lns[:, tc2, 1:2]),
                    eng.drain(),
                    eng.tensor_scalar_mul(stc[:, 1:2], stc[:, 1:2], 1.0 / D),
                    eng.drain(),
                    eng.tensor_add(stc[:, 2:3], stc[:, 4:5], stc[:, 5:6]),
                    eng.drain(),
                    eng.tensor_mul(stc[:, 3:4], stc[:, 1:2], stc[:, 1:2]),
                    eng.drain(),
                    eng.scalar_tensor_tensor(stc[:, 3:4], stc[:, 2:3], 1.0 / D,
                                             stc[:, 3:4], ALU.mult, ALU.subtract),
                    eng.drain())[-1],
                    waits=[t_sq])
                t_rstd = sop(lambda eng: (
                    eng.activation(stc[:, 6:7], stc[:, 3:4], AF.Ln, bias=epsln[:, :]),
                    eng.drain(),
                    eng.activation(stc[:, 6:7], stc[:, 6:7], AF.Exp, scale=-0.5),
                    eng.drain())[-1],
                    waits=[t_st])
                t_ng = vop(lambda eng: (
                    eng.scalar_tensor_tensor(stc[:, 2:3], stc[:, 1:2], -1.0,
                                             stc[:, 6:7], ALU.mult, ALU.mult),
                    eng.drain())[-1],
                    waits=[t_rstd])
                t_nm = sop(lambda eng, rs=rs, tc2=tc2: eng.activation(
                    r1b[:, rs, :], acc[:, tc2, :], AF.Identity,
                    scale=stc[:, 6:7], bias=stc[:, 2:3]),
                    waits=[t_ng, r1_guard.get(rs)])
                stc_guard = t_nm
                dma("sync",
                    y_out[:, :].rearrange("(c p) j -> p c j", p=128)[:, tcg, :],
                    r1b[:, rs, :], waits=[t_nm])
                t_yd = publish("sync")
                r1_guard[rs] = t_yd
                t_y[(hf, tc2)] = t_yd

            for e in range(NEXP):
                w1lo, w1hi = t_w1[(hf, e)]
                w2lo, w2hi = t_w2[(hf, e)]
                for tt in range(2):
                    t_pf_lo = None
                    t_pf_all = None
                    ge_last = None
                    for f in range(HFC):
                        bk = f % 2
                        wv = [w1lo if f < 8 else w1hi, t_he[(hf, 4 * tt + 3)]]
                        t_pf = mm([[(p[bk][:, :],
                                     wstream[:, dc, 128 * f:128 * f + 128],
                                     xh[:, dc, 512 * tt:512 * tt + 512])
                                    for dc in range(DC)]], waits=wv, pe_bank=bk)
                        if f == 7:
                            t_pf_lo = t_pf
                        if f == HFC - 1:
                            t_pf_all = t_pf
                        wg = [t_pf, t_rb1c, t_h1_guard]
                        if e == 0 and tt == 0:
                            wg += [t_qguse[(hf, 3)], t_sinuse[(hf, 7)],
                                   t_womm_last[hf]]
                        ge_last = sop(lambda eng, bk=bk, f=f, e=e:
                                      eng.activation(h1[:, f, :], p[bk][:, :],
                                                     AF.Gelu,
                                                     bias=rb1c[:, e, f:f + 1]),
                                      waits=wg)
                        bankg[bk] = ge_last
                    if tt == 1:
                        t_prod_lo[(hf, e)] = t_pf_lo
                        t_prod_all[(hf, e)] = t_pf_all
                    cons_last = None
                    cons_lo = None
                    for ds in range(2):
                        for tl in range(4):
                            tc2 = 4 * tt + tl
                            tcg = 8 * hf + tc2
                            bk = 2 + (4 * ds + tl) % 2
                            t_cmm = mm([[(p[bk][:, :], h1[:, f2, 128 * tl:128 * tl + 128],
                                          slotB[:, f2, 512 * ds:512 * ds + 512])
                                         for f2 in range(HFC)]],
                                       waits=[ge_last, w2lo if ds == 0 else w2hi],
                                       pe_bank=bk)
                            t_h1_guard = t_cmm
                            if ds == 0 and tl == 3:
                                cons_lo = t_cmm
                            cons_last = t_cmm
                            wcol = 1.0 if e == 0 else wts[:, tcg, e - 1:e]
                            t_mx = vop(lambda eng, bk=bk, tc2=tc2, ds=ds, wcol=wcol:
                                       eng.scalar_tensor_tensor(
                                           acc[:, tc2, 512 * ds:512 * ds + 512],
                                           p[bk][:, :], wcol,
                                           acc[:, tc2, 512 * ds:512 * ds + 512],
                                           ALU.mult, ALU.add),
                                       waits=[t_cmm])
                            bankg[bk] = t_mx
                    if tt == 1:
                        t_cons_lo[(hf, e)] = cons_lo
                        t_cons_all[(hf, e)] = cons_last
                    if e == NEXP - 1:
                        for tc2_ in range(4 * tt, 4 * tt + 4):
                            emit_tail(tc2_)
                if e < 8:
                    src1 = rw1_in[e, :, :]
                    src2 = rw2_in[e, :, :]
                    dma("gpsimd", wstream[:, :, 0:1024],
                        src1[:, 0:1024].rearrange("(c p) j -> p c j", p=128),
                        waits=[t_prod_lo[(hf, e)]])
                    a = publish("gpsimd")
                    dma("gpsimd", wstream[:, :, 1024:2048],
                        src1[:, 1024:2048].rearrange("(c p) j -> p c j", p=128),
                        waits=[t_prod_all[(hf, e)]])
                    bq = publish("gpsimd")
                    dma("gpsimd", slotB[:, :, 0:512],
                        src2[:, 0:512].rearrange("(c p) j -> p c j", p=128),
                        waits=[t_cons_lo[(hf, e)]])
                    c_ = publish("gpsimd")
                    dma("gpsimd", slotB[:, :, 512:1024],
                        src2[:, 512:1024].rearrange("(c p) j -> p c j", p=128),
                        waits=[t_cons_all[(hf, e)]])
                    d_ = publish("gpsimd")
                    t_w1[(hf, e + 1)] = (a, bq)
                    t_w2[(hf, e + 1)] = (c_, d_)
            if hf == 0:
                dma("gpsimd", slotB[:, 0:8, :],
                    wo_in[:, :].rearrange("(c p) j -> p c j", p=128),
                    waits=[t_cons_all[(0, 8)]])
                t_wo1_tok = publish("gpsimd")

            if phases < 9 and hf == 0:
                return finish()

        return finish()


def host_inputs(inputs, dbg=()):
    x = np.ascontiguousarray(np.asarray(inputs["L0_attn_x"], dtype=np.float32))
    sb1 = np.asarray(inputs["L0_ffn_sb1"], dtype=np.float32)
    rb1 = np.asarray(inputs["L0_ffn_rb1"], dtype=np.float32)
    sb2 = np.asarray(inputs["L0_ffn_sb2"], dtype=np.float32)
    rb2 = np.asarray(inputs["L0_ffn_rb2"], dtype=np.float32)

    ident = np.eye(128, dtype=np.float32)
    ii = np.arange(128)[:, None]
    jj = np.arange(256)[None, :]
    maskr = np.where(jj <= ii + 128, 0.0, NEG).astype(np.float32)
    mask_sf = np.where((jj >= 128) & (jj - 128 <= ii), 0.0, NEG).astype(np.float32)
    maskr = np.tile(maskr, (1, 2))
    mask_sf = np.tile(mask_sf, (1, 2))
    esum = np.zeros((128, NP * 16), np.float32)
    for ci in range(NP):
        for r in range(128):
            g = 2 * ci + r // 64
            esum[r, 16 * ci + g] = 1.0
    ebc = np.zeros((GH, NP * 128), np.float32)
    for ci in range(NP):
        for r in range(128):
            g = 2 * ci + r // 64
            ebc[g, 128 * ci + r] = 1.0
    rb1c = np.zeros((128, NEXP * HFC), np.float32)
    for cc in range(HFC):
        rb1c[:, 0 * HFC + cc] = sb1[128 * cc:128 * cc + 128]
        for e in range(8):
            rb1c[:, (e + 1) * HFC + cc] = rb1[e, 128 * cc:128 * cc + 128]
    rb9 = np.concatenate([rb2, sb2[None, :]], axis=0).astype(np.float32)

    shared = {
        "wq_in": np.ascontiguousarray(inputs["L0_attn_Wq"], dtype=np.float32),
        "wk_in": np.ascontiguousarray(inputs["L0_attn_Wk"], dtype=np.float32),
        "wv_in": np.ascontiguousarray(inputs["L0_attn_Wv"], dtype=np.float32),
        "wo_in": np.ascontiguousarray(inputs["L0_attn_Wo"], dtype=np.float32),
        "bo_in": np.asarray(inputs["L0_attn_bo"], dtype=np.float32).reshape(1, D),
        "rw_in": np.ascontiguousarray(inputs["L0_ffn_router_w"], dtype=np.float32),
        "rtb_in": np.asarray(inputs["L0_ffn_router_b"], dtype=np.float32).reshape(1, 8),
        "sw1_in": np.ascontiguousarray(inputs["L0_ffn_sw1"], dtype=np.float32),
        "sw2_in": np.ascontiguousarray(inputs["L0_ffn_sw2"], dtype=np.float32),
        "rw1_in": np.ascontiguousarray(inputs["L0_ffn_rw1"], dtype=np.float32),
        "rw2_in": np.ascontiguousarray(inputs["L0_ffn_rw2"], dtype=np.float32),
        "rb1c_in": rb1c, "rb9_in": rb9, "ident_in": ident,
        "esum_in": esum, "ebc_in": ebc, "maskr_in": maskr,
    }
    in_maps = []
    for c in range(8):
        sq, qq = c // 4, c % 4
        sl = x[sq, TOK * qq:TOK * qq + TOK]
        halo = (np.zeros((HALO, D), np.float32) if qq == 0
                else x[sq, TOK * qq - HALO:TOK * qq])
        m = dict(shared)
        m["x_in"] = np.ascontiguousarray(np.concatenate([halo, sl], axis=0))
        m["mask0_in"] = mask_sf if qq == 0 else maskr
        pmv = np.zeros((128, 4), np.float32)
        pmv[:, :qq] = 1.0
        m["pm_in"] = pmv
        in_maps.append(m)
    return in_maps


def kernel(**inputs):
    nc = build()
    in_maps = host_inputs(inputs)
    res = run_bass_kernel_spmd(nc, in_maps, core_ids=list(range(8)))
    y = np.stack([np.asarray(res.results[c]["y"]) for c in range(8)])
    return y.reshape(2, 8192, D).astype(np.float32)
